# version 03
# baseline (speedup 1.0000x reference)
import numpy as np
import concourse.bass as bass
import concourse.mybir as mybir
from concourse.bass_utils import run_bass_kernel_spmd

F32 = mybir.dt.float32
I32 = mybir.dt.int32
BF16 = mybir.dt.bfloat16
AX = mybir.AxisListType.X
AF = mybir.ActivationFunctionType
ALU = mybir.AluOpType

NCORES = 8
D = 1024
DC = D // 128
SEQ = 16384
TOK = SEQ // NCORES
W = 128
NEG = -3.4028234663852886e38
EPS = 1e-5


class Sched:
    ENGINES = ("sync", "vector", "scalar", "tensor", "gpsimd")

    def __init__(self, nc):
        self.nc = nc
        self.ops = {e: [] for e in self.ENGINES}
        self.semval = {}

    def emit(self, eng, fn):
        self.ops[eng].append(fn)

    def inc(self, sem, n=1):
        v = self.semval.get(sem.name, 0) + n
        self.semval[sem.name] = v
        return v

    def val(self, sem):
        return self.semval.get(sem.name, 0)

    def wait(self, eng, sem, v):
        self.emit(eng, lambda e, s=sem, vv=v: e.wait_ge(s, vv))

    def dma(self, eng, sem, out, in_):
        self.inc(sem, 16)
        self.emit(eng, lambda e, o=out, i=in_, s=sem: e.dma_start(out=o, in_=i).then_inc(s, 16))
        return self.val(sem)

    def op(self, eng, fn, sem=None):
        if sem is None:
            self.emit(eng, fn)
            return None
        self.inc(sem, 1)
        self.emit(eng, lambda e, f=fn, s=sem: f(e).then_inc(s, 1))
        return self.val(sem)

    def drain(self, eng, sem=None):
        if sem is None:
            self.emit(eng, lambda e: e.drain())
            return None
        self.inc(sem, 1)
        self.emit(eng, lambda e, s=sem: e.drain().then_inc(s, 1))
        return self.val(sem)

    def run(self, block):
        for eng in self.ENGINES:
            ops = self.ops[eng]
            if not ops:
                continue
            dec = getattr(block, eng)

            def body(e, ops=ops):
                for f in ops:
                    f(e)
            dec(body)


def mk_sems(ctx, nc, names):
    return {n: ctx.enter_context(nc.semaphore(n)) for n in names}


def tiles512(T):
    out = []
    off = 0
    while off < T:
        ln = min(512, T - off)
        out.append((off, ln))
        off += ln
    return out


def emit_stats(S, sems, tag, x, T, coff, psA, psB, sqb, rows, onescol, tmprow, tmprow2):
    sq_done = sems[tag + "_sd"]
    sq_free = sems[tag + "_sf"]
    fin = sems[tag]
    sq_free0 = S.val(sq_free)
    k = 0
    for (toff, tlen) in tiles512(T):
        for c in range(DC):
            buf = sqb[k % 2]
            if k >= 2:
                S.wait("scalar", sq_free, sq_free0 + k - 1)
            S.op("scalar", lambda s, c=c, toff=toff, tlen=tlen, buf=buf: s.square(
                out=buf[:, :tlen], in_=x[:, coff + c * T + toff: coff + c * T + toff + tlen]),
                 sq_done)
            S.wait("tensor", sq_done, S.val(sq_done))
            st = (c == 0)
            sp = (c == DC - 1)
            S.emit("tensor", lambda e, c=c, toff=toff, tlen=tlen, st=st, sp=sp: e.matmul(
                psA[:, :tlen], lhsT=onescol[:, 0:1],
                rhs=x[:, coff + c * T + toff: coff + c * T + toff + tlen],
                start=st, stop=sp))
            S.op("tensor", lambda e, buf=buf, tlen=tlen, st=st, sp=sp: e.matmul(
                psB[:, :tlen], lhsT=onescol[:, 0:1],
                rhs=buf[:, :tlen], start=st, stop=sp), sq_free)
            k += 1
        S.wait("vector", sq_free, sq_free0 + k)
        S.emit("vector", lambda v, toff=toff, tlen=tlen: v.tensor_scalar(
            out=rows[0:1, toff:toff + tlen], in0=psA[0:1, :tlen],
            scalar1=1.0 / D, scalar2=None, op0=ALU.mult))
        S.op("vector", lambda v, toff=toff, tlen=tlen: v.tensor_scalar(
            out=rows[0:1, T + toff: T + toff + tlen], in0=psB[0:1, :tlen],
            scalar1=1.0 / D, scalar2=None, op0=ALU.mult), fin)
        S.wait("tensor", fin, S.val(fin))
    tmp2 = tmprow2
    S.drain("vector")
    S.emit("vector", lambda v: v.tensor_mul(out=tmprow,
                                            in0=rows[0:1, 0:T], in1=rows[0:1, 0:T]))
    S.drain("vector")
    S.emit("vector", lambda v: v.tensor_sub(out=tmp2,
                                            in0=rows[0:1, T:2 * T], in1=tmprow))
    S.drain("vector")
    S.emit("vector", lambda v: v.tensor_scalar(out=tmprow, in0=tmp2,
                                               scalar1=EPS, scalar2=None, op0=ALU.add))
    S.drain("vector")
    S.op("vector", lambda v: v.reciprocal(out=tmp2, in_=tmprow), fin)
    S.wait("scalar", fin, S.val(fin))
    return S.op("scalar", lambda s: s.activation(out=rows[0:1, T:2 * T], in_=tmp2,
                                                 func=AF.Sqrt), fin)


def emit_bcast_rows(S, sems, tag, rows, T, dsts, psC, onesP, gate=None):
    br = sems[tag + "_br"]
    fin = sems[tag]
    if gate is not None:
        S.wait("tensor", gate[0], gate[1])
    k = 0
    fin0 = S.val(fin)
    for i, dst in enumerate(dsts):
        for (toff, tlen) in tiles512(T):
            pc = psC[k % 2]
            if k >= 2:
                S.wait("tensor", fin, fin0 + k - 1)
            S.op("tensor", lambda e, i=i, toff=toff, tlen=tlen, pc=pc: e.matmul(
                pc[:, :tlen], lhsT=onesP[0:1, :],
                rhs=rows[0:1, i * T + toff: i * T + toff + tlen],
                start=True, stop=True), br)
            S.wait("vector", br, S.val(br))
            S.op("vector", lambda v, dst=dst, toff=toff, tlen=tlen, pc=pc: v.tensor_copy(
                out=dst[:, toff:toff + tlen], in_=pc[:, :tlen]), fin)
            k += 1
    return S.val(fin)


def emit_mod_norm(S, x, h, T, xoff, hoff, mb, rb, ca, sa, sem=None):
    for c in range(DC):
        S.emit("vector", lambda v, c=c: v.tensor_sub(
            out=h[:, hoff + c * T: hoff + (c + 1) * T],
            in0=x[:, xoff + c * T: xoff + (c + 1) * T], in1=mb[:, :T]))
        S.drain("vector")
        S.emit("vector", lambda v, c=c: v.tensor_mul(
            out=h[:, hoff + c * T: hoff + (c + 1) * T],
            in0=h[:, hoff + c * T: hoff + (c + 1) * T], in1=rb[:, :T]))
        S.drain("vector")
        S.op("vector", lambda v, c=c: v.tensor_scalar(
            out=h[:, hoff + c * T: hoff + (c + 1) * T],
            in0=h[:, hoff + c * T: hoff + (c + 1) * T],
            scalar1=ca[c], scalar2=sa[c], op0=ALU.mult, op1=ALU.add), sem)
        S.drain("vector")


def build_L2():
    nc = bass.Bass("TRN2", target_bir_lowering=False, debug=False)
    x1_d = nc.dram_tensor("x1T", [D, TOK], F32, kind="ExternalInput").ap()
    e_d = nc.dram_tensor("Eslot", [NCORES, 128, DC], F32, kind="ExternalInput").ap()
    mod_d = nc.dram_tensor("mod1f", [6, 128, DC], F32, kind="ExternalInput").ap()
    a_d = nc.dram_tensor("Afm", [128, DC], F32, kind="ExternalInput").ap()
    bc_d = nc.dram_tensor("BC", [D, D], F32, kind="ExternalInput").ap()
    id_d = nc.dram_tensor("ident", [128, 128], F32, kind="ExternalInput").ap()
    x2_d = nc.dram_tensor("x2T", [D, TOK], F32, kind="ExternalOutput").ap()
    h2_d = nc.dram_tensor("h2T", [D, TOK], F32, kind="ExternalOutput").ap()
    m1_d = nc.dram_tensor("m1", [1, 1], F32, kind="ExternalOutput").ap()

    from contextlib import ExitStack
    with ExitStack() as ctx:
        sb = lambda name, shape, dt=F32: ctx.enter_context(nc.sbuf_tensor(name, shape, dt))
        T = TOK
        bufA = sb("bufA", [128, DC * T])
        bufB = sb("bufB", [128, DC * T])
        h1t = sb("h1t", [128, T])
        bco = [sb(f"bco{i}", [128, DC * 128]) for i in range(2)]
        xo = [sb(f"xo{i}", [128, 512]) for i in range(2)]
        et = sb("et", [128, NCORES * DC])
        modt = sb("modt", [128, 6 * DC])
        capl = sb("capl", [128, 2 * DC])
        at = sb("at", [128, DC])
        a2k = sb("a2k", [128, DC])
        carry = sb("carry", [128, DC])
        onescol = sb("onescol", [128, 1])
        onesP = sb("onesP", [1, 128])
        rows = sb("rows", [1, 2 * T])
        mb = sb("mb", [128, T])
        rb = sb("rb", [128, T])
        sqb = [sb("sqb0", [128, 512]), sb("sqb1", [128, 512])]
        mx = sb("mx", [128, DC])
        mxt = sb("mxt", [128, 1])
        m1t = sb("m1t", [1, 1])
        ident = sb("identt", [128, 128])

        psA = ctx.enter_context(nc.psum_tensor("psA", [1, 512], F32))
        psB = ctx.enter_context(nc.psum_tensor("psB", [1, 512], F32))
        psC = [ctx.enter_context(nc.psum_tensor(f"psC{i}", [128, 512], F32)) for i in range(2)]
        psT = ctx.enter_context(nc.psum_tensor("psT", [1, 128], F32))

        semnames = ["ld", "ldbc", "vec", "mm", "mmf", "st", "stx"]
        for tag in ["st1", "st2"]:
            semnames += [tag, tag + "_sd", tag + "_sf", tag + "_rd"]
        for tag in ["bc1", "bc2"]:
            semnames += [tag, tag + "_br"]
        sems = mk_sems(ctx, nc, semnames)
        block = ctx.enter_context(nc.Block())
        S = Sched(nc)

        S.dma("sync", sems["ld"], bufA[:].rearrange("p (c t) -> p c t", c=DC),
              x1_d[:].rearrange("(c p) t -> p c t", p=128))
        S.dma("sync", sems["ld"], et[:].rearrange("p (n c) -> p n c", n=NCORES),
              e_d[:].rearrange("n p c -> p n c"))
        S.dma("sync", sems["ld"], modt[:].rearrange("p (m c) -> p m c", m=6),
              mod_d[:].rearrange("m p c -> p m c"))
        S.dma("sync", sems["ld"], at[:], a_d[:])
        S.dma("sync", sems["ld"], ident[:], id_d[:])
        v_ld = S.val(sems["ld"])

        S.emit("vector", lambda v: v.memset(onescol[:], 1.0))
        v_ones = S.op("vector", lambda v: v.memset(onesP[:], 1.0), sems["vec"])
        S.wait("vector", sems["ld"], v_ld)
        S.emit("vector", lambda v: v.tensor_scalar(out=capl[:, 0:DC], in0=modt[:, 1 * DC:2 * DC],
                                                   scalar1=1.0, scalar2=None, op0=ALU.add))
        S.emit("vector", lambda v: v.tensor_scalar(out=capl[:, DC:2 * DC], in0=modt[:, 4 * DC:5 * DC],
                                                   scalar1=1.0, scalar2=None, op0=ALU.add))
        S.drain("vector")

        S.emit("vector", lambda v: v.tensor_copy(out=a2k[:], in_=at[:]))
        for _ in range(11):
            S.drain("vector")
            S.emit("vector", lambda v: v.tensor_mul(out=a2k[:], in0=a2k[:], in1=a2k[:]))
        S.drain("vector")
        S.emit("vector", lambda v: v.memset(carry[:], 0.0))
        for s in range(NCORES):
            S.drain("vector")
            S.emit("vector", lambda v: v.tensor_mul(out=carry[:], in0=carry[:], in1=a2k[:]))
            S.drain("vector")
            S.emit("vector", lambda v: v.tensor_add(out=carry[:], in0=carry[:],
                                                    in1=et[:, s * DC:(s + 1) * DC]))
        S.drain("vector", sems["vec"])

        S.wait("scalar", sems["ld"], v_ld)
        S.wait("tensor", sems["vec"], v_ones)
        v_st1 = emit_stats(S, sems, "st1", bufA, T, 0, psA, psB, sqb, rows, onescol, mb[0:1, :T], rb[0:1, :T])
        emit_bcast_rows(S, sems, "bc1", rows, T, [mb, rb], psC, onesP, gate=(sems["st1"], v_st1))
        ca1 = [capl[:, c:c + 1] for c in range(DC)]
        sa1 = [modt[:, 0 * DC + c:0 * DC + c + 1] for c in range(DC)]
        emit_mod_norm(S, bufA, bufB, T, 0, 0, mb, rb, ca1, sa1)
        for c in range(DC):
            S.emit("vector", lambda v, c=c: v.tensor_copy(out=h1t[:, :T], in_=bufB[:, c * T:(c + 1) * T]))
            S.drain("vector")
            sc = S.op("vector", lambda v, c=c: v.tensor_tensor_scan(
                out=bufB[:, c * T:(c + 1) * T], data0=at[:, c:c + 1].broadcast_to([128, T]),
                data1=h1t[:, :T], initial=carry[:, c:c + 1], op0=ALU.mult, op1=ALU.add),
                sems["vec"])
            S.drain("vector")
        v_scan = sc

        S.wait("tensor", sems["vec"], v_scan)
        NT = T // 512
        k = 0
        for o in range(DC):
            if o >= 2:
                S.wait("sync", sems["mm"], (o - 1) * NT)
            S.dma("sync", sems["ldbc"], bco[o % 2][:].rearrange("p (c q) -> p c q", c=DC),
                  bc_d[:, o * 128:(o + 1) * 128].rearrange("(c p) q -> p c q", p=128))
            S.wait("tensor", sems["ldbc"], 16 * (o + 1))
            for t in range(NT):
                pc = psC[k % 2]
                if k >= 2:
                    S.wait("tensor", sems["mmf"], k - 1)
                for c in range(DC - 1):
                    S.emit("tensor", lambda e, o=o, t=t, c=c, pc=pc: e.matmul(
                        pc[:, :512], lhsT=bco[o % 2][:, c * 128:(c + 1) * 128],
                        rhs=bufB[:, c * T + t * 512: c * T + (t + 1) * 512],
                        start=(c == 0), stop=False))
                S.op("tensor", lambda e, o=o, t=t, pc=pc: e.matmul(
                    pc[:, :512], lhsT=bco[o % 2][:, (DC - 1) * 128:DC * 128],
                    rhs=bufB[:, (DC - 1) * T + t * 512: (DC - 1) * T + (t + 1) * 512],
                    start=False, stop=True), sems["mm"])
                S.wait("vector", sems["mm"], k + 1)
                if k >= 2:
                    S.wait("vector", sems["stx"], 16 * (k - 1))
                S.op("vector", lambda v, o=o, t=t, pc=pc, k=k: v.scalar_tensor_tensor(
                    out=xo[k % 2][:, :512], in0=pc[:, :512],
                    scalar=modt[:, 2 * DC + o:2 * DC + o + 1],
                    in1=bufA[:, o * T + t * 512: o * T + (t + 1) * 512],
                    op0=ALU.mult, op1=ALU.add), sems["mmf"])
                S.wait("sync", sems["mmf"], k + 1)
                S.dma("sync", sems["stx"], x2_d[o * 128:(o + 1) * 128, t * 512:(t + 1) * 512],
                      xo[k % 2][:, :512])
                k += 1
                v_bc = S.drain("vector", sems["vec"])

        S.wait("sync", sems["stx"], 16 * k)
        S.dma("sync", sems["ld"], bufB[:].rearrange("p (c t) -> p c t", c=DC),
              x2_d[:].rearrange("(c p) t -> p c t", p=128))
        v_ld2 = S.val(sems["ld"])
        S.wait("scalar", sems["ld"], v_ld2)
        S.wait("tensor", sems["ld"], v_ld2)
        S.wait("vector", sems["ld"], v_ld2)
        v_st2 = emit_stats(S, sems, "st2", bufB, T, 0, psA, psB, sqb, rows, onescol, mb[0:1, :T], rb[0:1, :T])
        emit_bcast_rows(S, sems, "bc2", rows, T, [mb, rb], psC, onesP, gate=(sems["st2"], v_st2))
        cb1 = [capl[:, DC + c:DC + c + 1] for c in range(DC)]
        sb1 = [modt[:, 3 * DC + c:3 * DC + c + 1] for c in range(DC)]
        emit_mod_norm(S, bufB, bufA, T, 0, 0, mb, rb, cb1, sb1, sem=sems["vec"])
        v_h2 = S.val(sems["vec"])
        for c in range(DC):
            S.emit("vector", lambda v, c=c: v.reduce_max(
                out=mx[:, c:c + 1], in_=bufA[:, c * T:(c + 1) * T], axis=AX,
                apply_absolute_value=True))
            S.drain("vector")
        v_mx = S.op("vector", lambda v: v.reduce_max(out=mxt[:, 0:1], in_=mx[:, 0:DC], axis=AX),
                    sems["vec"])
        S.wait("tensor", sems["vec"], v_mx)
        S.op("tensor", lambda e: e.transpose(psT[:, :], in_=mxt[:, 0:1], identity=ident[:]),
             sems["mm"])
        S.wait("vector", sems["mm"], S.val(sems["mm"]))
        v_fin = S.op("vector", lambda v: v.reduce_max(out=m1t[0:1, 0:1], in_=psT[0:1, :], axis=AX),
                     sems["vec"])

        S.wait("sync", sems["vec"], v_fin)
        S.dma("sync", sems["st"], h2_d[:].rearrange("(c p) t -> p c t", p=128),
              bufA[:].rearrange("p (c t) -> p c t", c=DC))
        S.dma("sync", sems["st"], m1_d[:], m1t[:])
        S.wait("sync", sems["st"], S.val(sems["st"]))

        S.run(block)
    return nc


def _fm(vec):
    return np.ascontiguousarray(np.asarray(vec, np.float32).reshape(DC, 128).T)


_L2_CACHE = {}


def _get_l2():
    if "nc" not in _L2_CACHE:
        _L2_CACHE["nc"] = build_L2()
    return _L2_CACHE["nc"]


def run_launch2(x1T_all, E_all, mod1f, A, BC):
    nc = _get_l2()
    ident = np.eye(128, dtype=np.float32)
    afm = _fm(A)
    in_maps = []
    for i in range(NCORES):
        eslot = np.zeros((NCORES, 128, DC), np.float32)
        for j in range(i):
            eslot[8 - i + j] = E_all[j]
        in_maps.append({
            "x1T": x1T_all[i], "Eslot": eslot, "mod1f": mod1f, "Afm": afm,
            "BC": BC, "ident": ident,
        })
    res = run_bass_kernel_spmd(nc, in_maps, core_ids=list(range(NCORES)))
    return res.results


def emit_round(S, x, out, it, t1, t2, n, sem=None):
    S.emit("vector", lambda v: v.tensor_copy(out=it[:, :n], in_=x[:, :n]))
    S.emit("vector", lambda v: v.tensor_scalar(out=t1[:, :n], in0=x[:, :n],
                                               scalar1=0.0, scalar2=None, op0=ALU.is_ge))
    S.drain("vector")
    S.emit("vector", lambda v: v.tensor_copy(out=t2[:, :n], in_=it[:, :n]))
    S.emit("vector", lambda v: v.tensor_scalar(out=t1[:, :n], in0=t1[:, :n],
                                               scalar1=2.0, scalar2=-1.0,
                                               op0=ALU.mult, op1=ALU.add))
    S.drain("vector")
    S.emit("vector", lambda v: v.tensor_sub(out=out[:, :n], in0=x[:, :n], in1=t2[:, :n]))
    S.drain("vector")
    S.emit("vector", lambda v: v.tensor_mul(out=out[:, :n], in0=out[:, :n], in1=t1[:, :n]))
    S.drain("vector")
    S.emit("vector", lambda v: v.tensor_scalar(out=out[:, :n], in0=out[:, :n],
                                               scalar1=0.5, scalar2=None, op0=ALU.is_ge))
    S.drain("vector")
    S.emit("vector", lambda v: v.tensor_mul(out=out[:, :n], in0=out[:, :n], in1=t1[:, :n]))
    S.drain("vector")
    return S.op("vector", lambda v: v.tensor_add(out=out[:, :n], in0=out[:, :n],
                                                 in1=t2[:, :n]), sem)


def build_mods():
    nc = bass.Bass("TRN2", target_bir_lowering=False, debug=False)
    c_d = nc.dram_tensor("cf", [128, DC], F32, kind="ExternalInput").ap()
    wm_d = nc.dram_tensor("wmod", [D, 6 * D], F32, kind="ExternalInput").ap()
    mod_d = nc.dram_tensor("modf", [6, 128, DC], F32, kind="ExternalOutput").ap()

    from contextlib import ExitStack
    with ExitStack() as ctx:
        sb = lambda name, shape, dt=F32: ctx.enter_context(nc.sbuf_tensor(name, shape, dt))
        ct = sb("ct", [128, DC])
        st = sb("st", [128, DC])
        wt = [sb(f"wt{i}", [128, 8 * 128]) for i in range(2)]
        mo = sb("mo", [128, 48])
        tm1 = sb("tm1", [128, 8])
        tm2 = sb("tm2", [128, 8])
        ps = ctx.enter_context(nc.psum_tensor("ps", [128, 512], F32))
        sems = mk_sems(ctx, nc, ["ld", "ldwA", "ldwB", "act", "mm", "vec", "st"])
        block = ctx.enter_context(nc.Block())
        S = Sched(nc)

        S.dma("sync", sems["ld"], ct[:], c_d[:])
        v_ld = S.val(sems["ld"])
        S.wait("scalar", sems["ld"], v_ld)
        v_act = S.op("scalar", lambda s: s.activation(out=st[:], in_=ct[:], func=AF.Silu),
                     sems["act"])
        S.wait("tensor", sems["act"], v_act)
        for j in range(48):
            buf = wt[j % 2]
            sld = sems["ldwA" if j % 2 == 0 else "ldwB"]
            if j >= 2:
                S.wait("sync", sems["mm"], j - 1)
            S.dma("sync", sld, buf[:].rearrange("p (c q) -> p c q", c=DC),
                  wm_d[:, j * 128:(j + 1) * 128].rearrange("(c p) q -> p c q", p=128))
            S.wait("tensor", sld, S.val(sld))
            for dc in range(DC - 1):
                S.emit("tensor", lambda e, j=j, dc=dc, buf=buf: e.matmul(
                    ps[:, 0:1], lhsT=buf[:, dc * 128:(dc + 1) * 128],
                    rhs=st[:, dc:dc + 1], start=(dc == 0), stop=False))
            S.op("tensor", lambda e, buf=buf: e.matmul(
                ps[:, 0:1], lhsT=buf[:, (DC - 1) * 128:DC * 128],
                rhs=st[:, DC - 1:DC], start=False, stop=True), sems["mm"])
            S.wait("vector", sems["mm"], S.val(sems["mm"]))
            S.op("vector", lambda v, j=j: v.tensor_copy(out=mo[:, j:j + 1], in_=ps[:, 0:1]),
                 sems["vec"])
            S.wait("tensor", sems["vec"], S.val(sems["vec"]))
        for lo in (16, 40):
            S.emit("vector", lambda v, lo=lo: v.tensor_scalar(
                out=tm1[:, :8], in0=mo[:, lo:lo + 8], scalar1=0.0, scalar2=None,
                op0=ALU.is_ge))
            S.drain("vector")
            S.emit("vector", lambda v: v.tensor_scalar(
                out=tm1[:, :8], in0=tm1[:, :8], scalar1=2.0, scalar2=-1.0,
                op0=ALU.mult, op1=ALU.add))
            S.drain("vector")
            S.emit("vector", lambda v, lo=lo: v.tensor_mul(
                out=tm2[:, :8], in0=mo[:, lo:lo + 8], in1=tm1[:, :8]))
            S.drain("vector")
            S.emit("vector", lambda v: v.tensor_scalar(
                out=tm2[:, :8], in0=tm2[:, :8], scalar1=0.25, scalar2=None, op0=ALU.max))
            S.drain("vector")
            S.emit("vector", lambda v, lo=lo: v.tensor_mul(
                out=mo[:, lo:lo + 8], in0=tm1[:, :8], in1=tm2[:, :8]))
            S.drain("vector")
        v_fin = S.op("vector", lambda v: v.nop(), sems["vec"])
        S.wait("sync", sems["vec"], v_fin)
        S.dma("sync", sems["st"], mod_d[:].rearrange("m p c -> p m c"),
              mo[:, :48].rearrange("p (m c) -> p m c", m=6))
        S.wait("sync", sems["st"], S.val(sems["st"]))
        S.run(block)
    return nc


def build_attn():
    nc = bass.Bass("TRN2", target_bir_lowering=False, debug=False)
    x_d = nc.dram_tensor("xf", [D, TOK + 128], F32, kind="ExternalInput").ap()
    wq_d = nc.dram_tensor("wq", [D, D], F32, kind="ExternalInput").ap()
    wk_d = nc.dram_tensor("wk", [D, D], F32, kind="ExternalInput").ap()
    wv_d = nc.dram_tensor("wv", [D, D], F32, kind="ExternalInput").ap()
    wo_d = nc.dram_tensor("wo", [D, D], F32, kind="ExternalInput").ap()
    bq_d = nc.dram_tensor("bqf", [128, DC], F32, kind="ExternalInput").ap()
    bk_d = nc.dram_tensor("bkf", [128, DC], F32, kind="ExternalInput").ap()
    bo_d = nc.dram_tensor("bof", [128, DC], F32, kind="ExternalInput").ap()
    bv_d = nc.dram_tensor("bvrow", [1, D], F32, kind="ExternalInput").ap()
    mod_d = nc.dram_tensor("mod0f", [6, 128, DC], F32, kind="ExternalInput").ap()
    id_d = nc.dram_tensor("ident", [128, 128], F32, kind="ExternalInput").ap()
    mp_d = nc.dram_tensor("maskp", [128, 128], F32, kind="ExternalInput").ap()
    mc_d = nc.dram_tensor("maskc", [128, 128], F32, kind="ExternalInput").ap()
    x1_d = nc.dram_tensor("x1f", [D, TOK], F32, kind="ExternalOutput").ap()

    from contextlib import ExitStack
    with ExitStack() as ctx:
        sb = lambda name, shape, dt=F32: ctx.enter_context(nc.sbuf_tensor(name, shape, dt))
        CT = 256
        NCH = TOK // CT
        XT = CT + 128
        wq = sb("wqs", [128, DC * 1024])
        wk = sb("wks", [128, DC * 1024])
        wv = sb("wvs", [128, DC * 1024])
        wo = sb("wos", [128, DC * 1024])
        xh = sb("xh", [128, DC * XT])
        qw = sb("qw", [128, DC * 128])
        khb = sb("khb", [128, 3 * 16 * 128])
        vtm = sb("vtm", [128, 3 * 1024])
        atm = sb("atm", [128, 1024])
        afm = sb("afm", [128, 1024])
        x1w = sb("x1w", [128, 1024])
        rr = sb("rr", [1, 128])
        maskp = sb("maskpt", [128, 128])
        maskc = sb("maskct", [128, 128])
        ident = sb("identt", [128, 128])
        modt = sb("modt", [128, 6 * DC])
        capl = sb("capl", [128, DC])
        bqf = sb("bqfs", [128, DC])
        bkf = sb("bkfs", [128, DC])
        bof = sb("bofs", [128, DC])
        bvr = sb("bvr", [1, D])
        onescol = sb("onescol", [128, 1])
        onesP = sb("onesP", [1, 128])
        rows = sb("rows", [1, 2 * XT])
        mb = sb("mb", [128, XT])
        rb = sb("rb", [128, XT])
        sqb = [sb("sqb0", [128, 384]), sb("sqb1", [128, 384])]
        xr = qw
        esb = atm[:, 0:256]
        pnsb = atm[:, 256:512]

        pb = [ctx.enter_context(nc.psum_tensor(f"pb{i}", [128, 512], F32)) for i in range(8)]
        psA = pb[6][0:1, 0:512]
        psB = pb[7][0:1, 0:512]

        semnames = ["ld", "ldx", "ldxr", "vec", "mm", "sc", "st", "ck"]
        for tag in ["s1"]:
            semnames += [tag, tag + "_sd", tag + "_sf", tag + "_rd"]
        semnames += ["b1", "b1_br"]
        sems = mk_sems(ctx, nc, semnames)
        block = ctx.enter_context(nc.Block())
        S = Sched(nc)

        for w_sb, w_dd in [(wq, wq_d), (wk, wk_d), (wv, wv_d), (wo, wo_d)]:
            S.dma("sync", sems["ld"], w_sb[:].rearrange("p (c q) -> p c q", c=DC),
                  w_dd[:].rearrange("(c p) q -> p c q", p=128))
        S.dma("sync", sems["ld"], modt[:].rearrange("p (m c) -> p m c", m=6),
              mod_d[:].rearrange("m p c -> p m c"))
        for t_sb, t_dd in [(bqf, bq_d), (bkf, bk_d), (bof, bo_d), (bvr, bv_d),
                           (maskp, mp_d), (maskc, mc_d), (ident, id_d)]:
            S.dma("sync", sems["ld"], t_sb[:], t_dd[:])
        v_ld = S.val(sems["ld"])
        S.emit("vector", lambda v: v.memset(onescol[:], 1.0))
        S.emit("vector", lambda v: v.memset(onesP[:], 1.0))
        S.emit("vector", lambda v: v.memset(khb[:], 0.0))
        S.wait("vector", sems["ld"], v_ld)
        S.emit("vector", lambda v: v.tensor_scalar(out=capl[:], in0=modt[:, DC:2 * DC],
                                                   scalar1=1.0, scalar2=None, op0=ALU.add))
        v_pre = S.drain("vector", sems["vec"])
        S.wait("tensor", sems["vec"], v_pre)
        S.wait("scalar", sems["vec"], v_pre)
        S.wait("scalar", sems["ld"], v_ld)

        ca0 = [capl[:, c:c + 1] for c in range(DC)]
        sa0 = [modt[:, 0 * DC + c:0 * DC + c + 1] for c in range(DC)]

        for ci in range(NCH):
            v_ck = S.val(sems["ck"])
            for eng in ("sync", "vector", "scalar", "tensor"):
                S.wait(eng, sems["ck"], v_ck)
            S.dma("sync", sems["ldx"], xh[:].rearrange("p (c t) -> p c t", c=DC),
                  x_d[:, ci * CT: ci * CT + XT].rearrange("(c p) t -> p c t", p=128))
            v_ldx = S.val(sems["ldx"])
            for eng in ("vector", "scalar", "tensor"):
                S.wait(eng, sems["ldx"], v_ldx)

            v_st = emit_stats(S, sems, "s1", xh, XT, 0, psA, psB, sqb, rows,
                              onescol, mb[0:1, :XT], rb[0:1, :XT])
            emit_bcast_rows(S, sems, "b1", rows, XT, [mb, rb], [pb[4], pb[5]],
                            onesP, gate=(sems["s1"], v_st))
            emit_mod_norm(S, xh, xh, XT, 0, 0, mb, rb, ca0, sa0)
            v_h = S.drain("vector", sems["vec"])
            S.wait("tensor", sems["vec"], v_h)
            S.wait("scalar", sems["vec"], v_h)

            for win in range(3):
                hoff = win * 128
                slot = win
                for oc in range(DC):
                    if not (ci == 0 and win == 0 and oc == 0):
                        S.wait("tensor", sems["vec"], S.val(sems["vec"]))
                    for dc in range(DC - 1):
                        S.emit("tensor", lambda e, dc=dc, oc=oc, hoff=hoff: e.matmul(
                            pb[0][:, 0:128],
                            lhsT=wk[:, dc * 1024 + oc * 128: dc * 1024 + (oc + 1) * 128],
                            rhs=xh[:, dc * XT + hoff: dc * XT + hoff + 128],
                            start=(dc == 0), stop=False))
                    S.op("tensor", lambda e, oc=oc, hoff=hoff: e.matmul(
                        pb[0][:, 0:128],
                        lhsT=wk[:, (DC - 1) * 1024 + oc * 128: (DC - 1) * 1024 + (oc + 1) * 128],
                        rhs=xh[:, (DC - 1) * XT + hoff: (DC - 1) * XT + hoff + 128],
                        start=False, stop=True), sems["mm"])
                    S.wait("vector", sems["mm"], S.val(sems["mm"]))
                    ke = slot * 16 * 128 + (oc * 2) * 128
                    ko = slot * 16 * 128 + (oc * 2 + 1) * 128
                    S.emit("vector", lambda v, oc=oc, ke=ke: v.tensor_scalar(
                        out=khb[0:64, ke:ke + 128], in0=pb[0][0:64, 0:128],
                        scalar1=bkf[0:64, oc:oc + 1], scalar2=None, op0=ALU.add))
                    S.op("vector", lambda v, oc=oc, ko=ko: v.tensor_scalar(
                        out=khb[64:128, ko:ko + 128], in0=pb[0][64:128, 0:128],
                        scalar1=bkf[64:128, oc:oc + 1], scalar2=None, op0=ALU.add),
                        sems["vec"])
                S.wait("tensor", sems["vec"], S.val(sems["vec"]))
                for half in range(2):
                    pv = pb[1 + half]
                    for dc in range(DC):
                        S.emit("tensor", lambda e, dc=dc, hoff=hoff, half=half, pv=pv: e.matmul(
                            pv[:, 0:512],
                            lhsT=xh[:, dc * XT + hoff: dc * XT + hoff + 128],
                            rhs=wv[:, dc * 1024 + half * 512: dc * 1024 + half * 512 + 512],
                            start=(dc == 0), stop=False))
                    S.op("tensor", lambda e, half=half, pv=pv: e.matmul(
                        pv[:, 0:512], lhsT=onesP[:, :],
                        rhs=bvr[:, half * 512: half * 512 + 512],
                        start=False, stop=True), sems["mm"])
                S.wait("vector", sems["mm"], S.val(sems["mm"]))
                for half in range(2):
                    S.op("vector", lambda v, half=half, slot=slot: v.tensor_copy(
                        out=vtm[:, slot * 1024 + half * 512: slot * 1024 + half * 512 + 512],
                        in_=pb[1 + half][:, 0:512]), sems["vec"])
                S.wait("tensor", sems["vec"], S.val(sems["vec"]))

            for w in range(2):
                gwin = ci * 2 + w
                hoff = (w + 1) * 128
                sself = w + 1
                sprev = w
                tok0 = (ci * 2 + w) * 128
                for oc in range(DC):
                    if not (w == 0 and oc == 0):
                        S.wait("tensor", sems["vec"], S.val(sems["vec"]))
                    for dc in range(DC - 1):
                        S.emit("tensor", lambda e, dc=dc, oc=oc, hoff=hoff: e.matmul(
                            pb[0][:, 0:128],
                            lhsT=wq[:, dc * 1024 + oc * 128: dc * 1024 + (oc + 1) * 128],
                            rhs=xh[:, dc * XT + hoff: dc * XT + hoff + 128],
                            start=(dc == 0), stop=False))
                    S.op("tensor", lambda e, oc=oc, hoff=hoff: e.matmul(
                        pb[0][:, 0:128],
                        lhsT=wq[:, (DC - 1) * 1024 + oc * 128: (DC - 1) * 1024 + (oc + 1) * 128],
                        rhs=xh[:, (DC - 1) * XT + hoff: (DC - 1) * XT + hoff + 128],
                        start=False, stop=True), sems["mm"])
                    S.wait("vector", sems["mm"], S.val(sems["mm"]))
                    S.op("vector", lambda v, oc=oc: v.tensor_scalar(
                        out=qw[:, oc * 128:(oc + 1) * 128], in0=pb[0][:, 0:128],
                        scalar1=bqf[:, oc:oc + 1], scalar2=None, op0=ALU.add), sems["vec"])
                S.wait("tensor", sems["vec"], S.val(sems["vec"]))
                for h in range(16):
                    c, p = h // 2, h % 2
                    kprev = sprev * 16 * 128 + (2 * c + p) * 128
                    kself = sself * 16 * 128 + (2 * c + p) * 128
                    S.emit("tensor", lambda e, kprev=kprev, c=c: e.matmul(
                        pb[3][:, 0:128], lhsT=khb[:, kprev:kprev + 128],
                        rhs=qw[:, c * 128:(c + 1) * 128], start=True, stop=True))
                    S.op("tensor", lambda e, kself=kself, c=c: e.matmul(
                        pb[3][:, 128:256], lhsT=khb[:, kself:kself + 128],
                        rhs=qw[:, c * 128:(c + 1) * 128], start=True, stop=True), sems["mm"])
                    S.wait("scalar", sems["mm"], S.val(sems["mm"]))
                    v_e = S.op("scalar", lambda s: s.activation(
                        out=esb[:], in_=pb[3][:, 0:256], func=AF.Exp, scale=0.125),
                        sems["sc"])
                    S.wait("vector", sems["sc"], v_e)
                    if ci == 0 and w == 0:
                        S.emit("vector", lambda v: v.tensor_mul(
                            out=esb[:, 0:128], in0=esb[:, 0:128], in1=maskp[:]))
                        S.drain("vector")
                    S.op("vector", lambda v: v.tensor_mul(
                        out=esb[:, 128:256], in0=esb[:, 128:256], in1=maskc[:]),
                        sems["vec"])
                    S.wait("tensor", sems["vec"], S.val(sems["vec"]))
                    S.emit("tensor", lambda e: e.matmul(
                        pb[3][0:1, 384:512], lhsT=onescol[:, :],
                        rhs=esb[:, 0:128], start=True, stop=False))
                    S.op("tensor", lambda e: e.matmul(
                        pb[3][0:1, 384:512], lhsT=onescol[:, :],
                        rhs=esb[:, 128:256], start=False, stop=True), sems["mm"])
                    S.wait("vector", sems["mm"], S.val(sems["mm"]))
                    v_rr = S.op("vector", lambda v: v.reciprocal(
                        out=rr[:], in_=pb[3][0:1, 384:512]), sems["vec"])
                    S.wait("tensor", sems["vec"], v_rr)
                    S.op("tensor", lambda e: e.matmul(
                        pb[3][:, 256:384], lhsT=onesP[:, :], rhs=rr[:],
                        start=True, stop=True), sems["mm"])
                    S.wait("vector", sems["mm"], S.val(sems["mm"]))
                    S.emit("vector", lambda v: v.tensor_mul(
                        out=pnsb[:, 0:128], in0=esb[:, 0:128], in1=pb[3][:, 256:384]))
                    S.op("vector", lambda v: v.tensor_mul(
                        out=pnsb[:, 128:256], in0=esb[:, 128:256], in1=pb[3][:, 256:384]),
                        sems["vec"])
                    S.wait("tensor", sems["vec"], S.val(sems["vec"]))
                    bank = pb[4 + h // 8]
                    co = (h % 8) * 64
                    S.emit("tensor", lambda e, bank=bank, co=co, sprev=sprev, h=h: e.matmul(
                        bank[:, co:co + 64], lhsT=pnsb[:, 0:128],
                        rhs=vtm[:, sprev * 1024 + h * 64: sprev * 1024 + h * 64 + 64],
                        start=True, stop=False))
                    S.op("tensor", lambda e, bank=bank, co=co, sself=sself, h=h: e.matmul(
                        bank[:, co:co + 64], lhsT=pnsb[:, 128:256],
                        rhs=vtm[:, sself * 1024 + h * 64: sself * 1024 + h * 64 + 64],
                        start=False, stop=True), sems["mm"])
                S.wait("vector", sems["mm"], S.val(sems["mm"]))
                for half in range(2):
                    S.op("vector", lambda v, half=half: v.tensor_copy(
                        out=atm[:, half * 512:(half + 1) * 512], in_=pb[4 + half][:, 0:512]),
                        sems["vec"])
                S.wait("tensor", sems["vec"], S.val(sems["vec"]))
                for hc in range(DC):
                    S.op("tensor", lambda e, hc=hc: e.transpose(
                        pb[6][:, 0:128], in_=atm[:, hc * 128:(hc + 1) * 128],
                        identity=ident[:]), sems["mm"])
                    S.wait("vector", sems["mm"], S.val(sems["mm"]))
                    S.op("vector", lambda v, hc=hc: v.tensor_copy(
                        out=afm[:, hc * 128:(hc + 1) * 128], in_=pb[6][:, 0:128]),
                        sems["vec"])
                    S.wait("tensor", sems["vec"], S.val(sems["vec"]))
                S.wait("gpsimd", sems["vec"], S.val(sems["vec"]))
                S.wait("gpsimd", sems["mm"], S.val(sems["mm"]))
                S.dma("gpsimd", sems["ldxr"], xr[:, 0:1024].rearrange("p (c t) -> p c t", c=DC),
                      x_d[:, 128 + tok0: 128 + tok0 + 128].rearrange("(c p) t -> p c t", p=128))
                v_xr = S.val(sems["ldxr"])
                for oc in range(DC):
                    if oc > 0:
                        S.wait("tensor", sems["vec"], S.val(sems["vec"]))
                    for dc in range(DC - 1):
                        S.emit("tensor", lambda e, dc=dc, oc=oc: e.matmul(
                            pb[7][:, 0:128],
                            lhsT=wo[:, dc * 1024 + oc * 128: dc * 1024 + (oc + 1) * 128],
                            rhs=afm[:, dc * 128:(dc + 1) * 128], start=(dc == 0), stop=False))
                    S.op("tensor", lambda e, oc=oc: e.matmul(
                        pb[7][:, 0:128],
                        lhsT=wo[:, (DC - 1) * 1024 + oc * 128: (DC - 1) * 1024 + (oc + 1) * 128],
                        rhs=afm[:, (DC - 1) * 128: DC * 128], start=False, stop=True),
                        sems["mm"])
                    S.wait("vector", sems["mm"], S.val(sems["mm"]))
                    S.op("vector", lambda v, oc=oc: v.tensor_scalar(
                        out=x1w[:, oc * 128:(oc + 1) * 128], in0=pb[7][:, 0:128],
                        scalar1=bof[:, oc:oc + 1], scalar2=modt[:, 2 * DC + oc:2 * DC + oc + 1],
                        op0=ALU.add, op1=ALU.mult), sems["vec"])
                S.wait("vector", sems["ldxr"], v_xr)
                S.drain("vector")
                v_x1 = S.op("vector", lambda v: v.tensor_add(
                    out=x1w[:], in0=x1w[:], in1=xr[:, 0:1024]), sems["vec"])
                S.wait("sync", sems["vec"], v_x1)
                S.dma("sync", sems["st"], x1_d[:, tok0:tok0 + 128].rearrange(
                    "(c p) t -> p c t", p=128), x1w[:].rearrange("p (c t) -> p c t", c=DC))
                S.wait("vector", sems["st"], S.val(sems["st"]))
                S.wait("tensor", sems["vec"], v_x1)
            S.op("tensor", lambda e: e.nop(), sems["ck"])
            S.op("vector", lambda v: v.nop(), sems["ck"])
            S.op("scalar", lambda s: s.nop(), sems["ck"])
            S.inc(sems["ck"], 1)
            S.emit("sync", lambda e, s=sems["ck"]: e.nop().then_inc(s, 1))

        S.wait("sync", sems["st"], S.val(sems["st"]))
        S.run(block)
    return nc


def build_moe():
    import os
    PH = int(os.environ.get("MOE_PH", "9"))
    nc = bass.Bass("TRN2", target_bir_lowering=False, debug=False)
    T = TOK
    x1_d = nc.dram_tensor("x1f", [D, T], F32, kind="ExternalInput").ap()
    m0_d = nc.dram_tensor("mod0f", [6, 128, DC], F32, kind="ExternalInput").ap()
    m1_d = nc.dram_tensor("mod1f", [6, 128, DC], F32, kind="ExternalInput").ap()
    gw_d = nc.dram_tensor("gwT", [D, 8], F32, kind="ExternalInput").ap()
    gb_d = nc.dram_tensor("gbrow", [1, 8], F32, kind="ExternalInput").ap()
    oh_d = nc.dram_tensor("onehot", [128, 1024], F32, kind="ExternalInput").ap()
    w1_d = nc.dram_tensor("w1T", [8, D, D], F32, kind="ExternalInput").ap()
    w3_d = nc.dram_tensor("w3T", [8, D, D], F32, kind="ExternalInput").ap()
    w2_d = nc.dram_tensor("w2T", [8, D, D], F32, kind="ExternalInput").ap()
    s1_d = nc.dram_tensor("sw1T", [D, 2048], F32, kind="ExternalInput").ap()
    s3_d = nc.dram_tensor("sw3T", [D, 2048], F32, kind="ExternalInput").ap()
    s2_d = nc.dram_tensor("sw2T", [2048, D], F32, kind="ExternalInput").ap()
    a_d = nc.dram_tensor("Afm", [128, DC], F32, kind="ExternalInput").ap()
    id_d = nc.dram_tensor("ident", [128, 128], F32, kind="ExternalInput").ap()
    x2_d = nc.dram_tensor("x2f", [D, T], F32, kind="ExternalOutput").ap()
    e_d = nc.dram_tensor("Ei", [128, DC], F32, kind="ExternalOutput").ap()

    from contextlib import ExitStack
    with ExitStack() as ctx:
        sb = lambda name, shape, dt=F32: ctx.enter_context(nc.sbuf_tensor(name, shape, dt))
        h2 = sb("h2", [128, DC * T])
        x2 = sb("x2s", [128, DC * T])
        gar = sb("gar", [128, 12288])
        gatesE = sb("gatesE", [128, T])
        w1t = [sb(f"w1t{i}", [128, DC * 128]) for i in range(2)]
        w3t = [sb(f"w3t{i}", [128, DC * 128]) for i in range(2)]
        w2t = [sb(f"w2t{i}", [128, 4 * 128]) for i in range(2)]
        silt = sb("silt", [128, 512])
        at = sb("ats", [128, DC])
        carry = sb("carry", [128, DC])
        modt0 = sb("modt0", [128, 6 * DC])
        modt1 = sb("modt1", [128, 6 * DC])
        capl = sb("capl", [128, 2 * DC])
        gwts = sb("gwts", [128, DC * 8])
        gbr = sb("gbr", [1, 8])
        biasbc = sb("biasbc", [128, 8])
        ident = sb("identt", [128, 128])
        onescol = sb("onescol", [128, 1])
        onesP = sb("onesP", [1, 128])

        rows = gar[0:1, 0:2 * T]
        mbb = gar[:, 4096:4096 + T]
        rbb = gar[:, 4096 + T:4096 + 2 * T]
        sqb = [gar[:, 8192:8704], gar[:, 8704:9216]]
        scant = gar[:, 0:512]
        esc = gar[0:8, 0:T]
        scn = gar[0:8, T:2 * T]
        rro = gar[0:1, 2 * T:2 * T + 512]
        gbce = gar[:, 8192:8192 + T]
        gbuf = gar[:, 0:8192]
        h1t = gar[:, 8192:8192 + DC * 512]
        oht = gar[:, 10240:11264]
        sctm = gar[:, 11264:11272]
        btm = gar[:, 11272:11280]
        ttm = gar[:, 11280:11288]
        m1c = gar[:, 11288:11289]

        pb = [ctx.enter_context(nc.psum_tensor(f"pb{i}", [128, 512], F32)) for i in range(8)]
        psA = pb[6][0:1, 0:512]
        psB = pb[7][0:1, 0:512]

        semnames = ["ld", "ldwA", "ldwB", "ldw2A", "ldw2B", "vec", "mm", "sc", "st"]
        for tag in ["s1", "s2"]:
            semnames += [tag, tag + "_sd", tag + "_sf", tag + "_rd"]
        for tag in ["b1", "b2"]:
            semnames += [tag, tag + "_br"]
        sems = mk_sems(ctx, nc, semnames)
        block = ctx.enter_context(nc.Block())
        S = Sched(nc)

        S.dma("sync", sems["ld"], x2[:].rearrange("p (c t) -> p c t", c=DC),
              x1_d[:].rearrange("(c p) t -> p c t", p=128))
        S.dma("sync", sems["ld"], modt0[:].rearrange("p (m c) -> p m c", m=6),
              m0_d[:].rearrange("m p c -> p m c"))
        S.dma("sync", sems["ld"], modt1[:].rearrange("p (m c) -> p m c", m=6),
              m1_d[:].rearrange("m p c -> p m c"))
        S.dma("sync", sems["ld"], gwts[:].rearrange("p (c q) -> p c q", c=DC),
              gw_d[:].rearrange("(c p) q -> p c q", p=128))
        S.dma("sync", sems["ld"], gbr[:], gb_d[:])
        S.dma("sync", sems["ld"], oht[:], oh_d[:])
        S.dma("sync", sems["ld"], ident[:], id_d[:])
        S.dma("sync", sems["ld"], at[:], a_d[:])
        v_ld = S.val(sems["ld"])
        S.emit("vector", lambda v: v.memset(onescol[:], 1.0))
        S.emit("vector", lambda v: v.memset(onesP[:], 1.0))
        S.emit("vector", lambda v: v.memset(carry[:], 0.0))
        S.emit("vector", lambda v: v.memset(gatesE[:], 0.0))
        S.wait("vector", sems["ld"], v_ld)
        S.emit("vector", lambda v: v.tensor_scalar(out=capl[:, 0:DC], in0=modt0[:, 4 * DC:5 * DC],
                                                   scalar1=1.0, scalar2=None, op0=ALU.add))
        S.emit("vector", lambda v: v.tensor_scalar(out=capl[:, DC:2 * DC], in0=modt1[:, DC:2 * DC],
                                                   scalar1=1.0, scalar2=None, op0=ALU.add))
        v_pre = S.drain("vector", sems["vec"])
        S.wait("tensor", sems["vec"], v_pre)
        S.wait("scalar", sems["vec"], v_pre)
        S.wait("scalar", sems["ld"], v_ld)
        S.wait("tensor", sems["ld"], v_ld)

        v_st = emit_stats(S, sems, "s1", x2, T, 0, psA, psB, sqb, rows,
                          onescol, mbb[0:1, :T], rbb[0:1, :T])
        emit_bcast_rows(S, sems, "b1", rows, T, [mbb, rbb], [pb[4], pb[5]],
                        onesP, gate=(sems["s1"], v_st))
        cb0 = [capl[:, c:c + 1] for c in range(DC)]
        sb0 = [modt0[:, 3 * DC + c:3 * DC + c + 1] for c in range(DC)]
        emit_mod_norm(S, x2, h2, T, 0, 0, mbb, rbb, cb0, sb0)
        v_h2 = S.drain("vector", sems["vec"])
        S.wait("tensor", sems["vec"], v_h2)
        S.wait("scalar", sems["vec"], v_h2)

        if PH < 2:
            v_fin = S.drain("vector", sems["vec"])
            S.wait("sync", sems["vec"], v_fin)
            S.dma("sync", sems["st"], x2_d[:].rearrange("(c p) t -> p c t", p=128),
                  h2[:].rearrange("p (c t) -> p c t", c=DC))
            S.dma("sync", sems["st"], e_d[:], carry[:])
            S.wait("sync", sems["st"], S.val(sems["st"]))
            S.run(block)
            return nc
        S.op("tensor", lambda e: e.matmul(pb[3][:, 0:8], lhsT=onesP[:, :], rhs=gbr[:, :],
                                          start=True, stop=True), sems["mm"])
        S.wait("vector", sems["mm"], S.val(sems["mm"]))
        v = S.op("vector", lambda v: v.tensor_copy(out=biasbc[:], in_=pb[3][:, 0:8]),
                 sems["vec"])
        S.wait("tensor", sems["vec"], v)
        NT = T // 512
        for tt in range(NT):
            for dc in range(DC - 1):
                S.emit("tensor", lambda e, dc=dc, tt=tt: e.matmul(
                    pb[2][0:8, 0:512], lhsT=gwts[:, dc * 8:(dc + 1) * 8],
                    rhs=h2[:, dc * T + tt * 512: dc * T + tt * 512 + 512],
                    start=(dc == 0), stop=False))
            S.op("tensor", lambda e, tt=tt: e.matmul(
                pb[2][0:8, 0:512], lhsT=gwts[:, (DC - 1) * 8:DC * 8],
                rhs=h2[:, (DC - 1) * T + tt * 512: (DC - 1) * T + tt * 512 + 512],
                start=False, stop=True), sems["mm"])
            S.wait("scalar", sems["mm"], S.val(sems["mm"]))
            v_e = S.op("scalar", lambda s, tt=tt: s.activation(
                out=esc[:, tt * 512:(tt + 1) * 512], in_=pb[2][0:8, 0:512], func=AF.Exp),
                sems["sc"])
            S.wait("tensor", sems["sc"], v_e)
            S.op("tensor", lambda e, tt=tt: e.matmul(
                pb[6][0:1, 0:512], lhsT=onescol[0:8, 0:1],
                rhs=esc[:, tt * 512:(tt + 1) * 512], start=True, stop=True), sems["mm"])
            S.wait("vector", sems["mm"], S.val(sems["mm"]))
            v_r = S.op("vector", lambda v: v.reciprocal(out=rro[:, 0:512],
                                                        in_=pb[6][0:1, 0:512]), sems["vec"])
            S.wait("tensor", sems["vec"], v_r)
            S.op("tensor", lambda e: e.matmul(
                pb[3][0:8, 0:512], lhsT=onesP[0:1, 0:8], rhs=rro[:, 0:512],
                start=True, stop=True), sems["mm"])
            S.wait("vector", sems["mm"], S.val(sems["mm"]))
            v_n = S.op("vector", lambda v, tt=tt: v.tensor_mul(
                out=scn[:, tt * 512:(tt + 1) * 512], in_=None) if False else v.tensor_tensor(
                out=scn[:, tt * 512:(tt + 1) * 512], in0=esc[:, tt * 512:(tt + 1) * 512],
                in1=pb[3][0:8, 0:512], op=ALU.mult), sems["vec"])
            S.wait("tensor", sems["vec"], v_n)
        for k in range(T // 128):
            S.op("tensor", lambda e, k=k: e.transpose(
                pb[3][:, 0:8], in_=scn[:, k * 128:(k + 1) * 128],
                identity=ident[0:8, 0:8]), sems["mm"])
            S.wait("vector", sems["mm"], S.val(sems["mm"]))
            S.emit("vector", lambda v: v.tensor_copy(out=sctm[:], in_=pb[3][:, 0:8]))
            S.drain("vector")
            S.emit("vector", lambda v: v.tensor_add(out=btm[:], in0=sctm[:], in1=biasbc[:]))
            S.drain("vector")
            S.emit("vector", lambda v: v.reduce_max(out=m1c[:], in_=btm[:], axis=AX))
            S.drain("vector")
            S.emit("vector", lambda v: v.tensor_scalar(out=ttm[:], in0=btm[:],
                                                       scalar1=m1c[:], scalar2=None,
                                                       op0=ALU.is_equal))
            S.drain("vector")
            S.emit("vector", lambda v: v.scalar_tensor_tensor(
                out=ttm[:], in0=ttm[:], scalar=-1e30, in1=btm[:],
                op0=ALU.mult, op1=ALU.add))
            S.drain("vector")
            S.emit("vector", lambda v: v.reduce_max(out=m1c[:], in_=ttm[:], axis=AX))
            S.drain("vector")
            S.emit("vector", lambda v: v.tensor_scalar(out=ttm[:], in0=btm[:],
                                                       scalar1=m1c[:], scalar2=None,
                                                       op0=ALU.is_ge))
            S.drain("vector")
            v_g = S.op("vector", lambda v: v.tensor_mul(out=sctm[:], in0=ttm[:], in1=sctm[:]),
                       sems["vec"])
            S.wait("tensor", sems["vec"], v_g)
            S.op("tensor", lambda e: e.transpose(
                pb[3][0:8, 128:256], in_=sctm[:], identity=ident[:]), sems["mm"])
            S.wait("vector", sems["mm"], S.val(sems["mm"]))
            v_c = S.op("vector", lambda v, k=k: v.tensor_copy(
                out=gatesE[0:8, k * 128:(k + 1) * 128], in_=pb[3][0:8, 128:256]), sems["vec"])
            S.wait("tensor", sems["vec"], v_c)

        if PH < 3:
            v_fin = S.drain("vector", sems["vec"])
            S.wait("sync", sems["vec"], v_fin)
            S.dma("sync", sems["st"], x2_d[0:8, 0:T], gatesE[:, :])
            S.dma("sync", sems["st"], e_d[:], carry[:])
            S.wait("sync", sems["st"], S.val(sems["st"]))
            S.run(block)
            return nc
        wfree = {0: 0, 1: 0}
        w2free = {0: 0, 1: 0}
        nld = 0
        NEXP = 10 if PH >= 4 else 1
        NOGB = os.environ.get("MOE_NOGB") == "1"
        NOW13 = os.environ.get("MOE_NOW13") == "1"
        NOW2 = os.environ.get("MOE_NOW2") == "1"
        for e in range(NEXP):
            if e < 8 and not NOGB:
                S.wait("tensor", sems["vec"], S.val(sems["vec"]))
                for tt in range(NT):
                    if tt > 0:
                        S.wait("tensor", sems["vec"], S.val(sems["vec"]))
                    S.op("tensor", lambda e_, e=e, tt=tt: e_.matmul(
                        pb[0][:, 0:512], lhsT=oht[:, e * 128:(e + 1) * 128],
                        rhs=gatesE[:, tt * 512:(tt + 1) * 512],
                        start=True, stop=True), sems["mm"])
                    S.wait("vector", sems["mm"], S.val(sems["mm"]))
                    S.op("vector", lambda v, tt=tt: v.tensor_copy(
                        out=gbce[:, tt * 512:(tt + 1) * 512], in_=pb[0][:, 0:512]),
                        sems["vec"])
                S.wait("tensor", sems["vec"], S.val(sems["vec"]))
            for ihalf in range(2):
                for ic4 in range(0 if NOW13 else 4):
                    par = nld % 2
                    nld += 1
                    ic = ihalf * 4 + ic4
                    sld = sems["ldwA" if par == 0 else "ldwB"]
                    S.wait("sync", sems["mm"], wfree[par])
                    if e < 8:
                        src1 = w1_d[e, :, ic * 128:(ic + 1) * 128]
                        src3 = w3_d[e, :, ic * 128:(ic + 1) * 128]
                    else:
                        co = (e - 8) * 1024 + ic * 128
                        src1 = s1_d[:, co:co + 128]
                        src3 = s3_d[:, co:co + 128]
                    S.dma("sync", sld, w1t[par][:].rearrange("p (c q) -> p c q", c=DC),
                          src1.rearrange("(c p) q -> p c q", p=128))
                    S.dma("sync", sld, w3t[par][:].rearrange("p (c q) -> p c q", c=DC),
                          src3.rearrange("(c p) q -> p c q", p=128))
                    v_lw = S.val(sld)
                    S.wait("tensor", sld, v_lw)
                    for tt in range(NT):
                        if not (e == 0 and ic == 0 and tt == 0):
                            S.wait("tensor", sems["vec"], S.val(sems["vec"]))
                        for dc in range(DC):
                            S.emit("tensor", lambda e_, dc=dc, tt=tt, par=par: e_.matmul(
                                pb[0][:, 0:512], lhsT=w1t[par][:, dc * 128:(dc + 1) * 128],
                                rhs=h2[:, dc * T + tt * 512: dc * T + tt * 512 + 512],
                                start=(dc == 0), stop=(dc == DC - 1)))
                        for dc in range(DC - 1):
                            S.emit("tensor", lambda e_, dc=dc, tt=tt, par=par: e_.matmul(
                                pb[1][:, 0:512], lhsT=w3t[par][:, dc * 128:(dc + 1) * 128],
                                rhs=h2[:, dc * T + tt * 512: dc * T + tt * 512 + 512],
                                start=(dc == 0), stop=False))
                        S.op("tensor", lambda e_, tt=tt, par=par: e_.matmul(
                            pb[1][:, 0:512], lhsT=w3t[par][:, (DC - 1) * 128:DC * 128],
                            rhs=h2[:, (DC - 1) * T + tt * 512: (DC - 1) * T + tt * 512 + 512],
                            start=False, stop=True), sems["mm"])
                        S.wait("scalar", sems["mm"], S.val(sems["mm"]))
                        v_s = S.op("scalar", lambda s: s.activation(
                            out=silt[:], in_=pb[0][:, 0:512], func=AF.Silu), sems["sc"])
                        S.wait("vector", sems["sc"], v_s)
                        go = ic4 * 2048 + tt * 512
                        S.emit("vector", lambda v, go=go: v.tensor_tensor(
                            out=gbuf[:, go:go + 512], in0=silt[:], in1=pb[1][:, 0:512],
                            op=ALU.mult))
                        if e < 8:
                            S.drain("vector")
                            S.emit("vector", lambda v, go=go, tt=tt: v.tensor_mul(
                                out=gbuf[:, go:go + 512], in0=gbuf[:, go:go + 512],
                                in1=gbce[:, tt * 512:(tt + 1) * 512]))
                        S.drain("vector", sems["vec"])
                    wfree[par] = S.val(sems["mm"])
                for oc in range(0 if NOW2 else DC):
                    par = oc % 2
                    sld2 = sems["ldw2A" if par == 0 else "ldw2B"]
                    S.wait("sync", sems["vec"], S.val(sems["vec"]))
                    S.wait("sync", sems["mm"], w2free[par])
                    if e < 8:
                        src2 = w2_d[e, ihalf * 512:(ihalf + 1) * 512, oc * 128:(oc + 1) * 128]
                    else:
                        ro = (e - 8) * 1024 + ihalf * 512
                        src2 = s2_d[ro:ro + 512, oc * 128:(oc + 1) * 128]
                    S.dma("sync", sld2, w2t[par][:].rearrange("p (c q) -> p c q", c=4),
                          src2.rearrange("(c p) q -> p c q", p=128))
                    v_lw = S.val(sld2)
                    S.wait("tensor", sld2, v_lw)
                    for tt in range(NT):
                        S.wait("tensor", sems["vec"], S.val(sems["vec"]))
                        for ic4 in range(3):
                            S.emit("tensor", lambda e_, ic4=ic4, tt=tt, par=par: e_.matmul(
                                pb[2][:, 0:512], lhsT=w2t[par][:, ic4 * 128:(ic4 + 1) * 128],
                                rhs=gbuf[:, ic4 * 2048 + tt * 512: ic4 * 2048 + tt * 512 + 512],
                                start=(ic4 == 0), stop=False))
                        S.op("tensor", lambda e_, tt=tt, par=par: e_.matmul(
                            pb[2][:, 0:512], lhsT=w2t[par][:, 3 * 128:4 * 128],
                            rhs=gbuf[:, 3 * 2048 + tt * 512: 3 * 2048 + tt * 512 + 512],
                            start=False, stop=True), sems["mm"])
                        S.wait("vector", sems["mm"], S.val(sems["mm"]))
                        S.op("vector", lambda v, oc=oc, tt=tt: v.scalar_tensor_tensor(
                            out=x2[:, oc * T + tt * 512: oc * T + tt * 512 + 512],
                            in0=pb[2][:, 0:512],
                            scalar=modt0[:, 5 * DC + oc:5 * DC + oc + 1],
                            in1=x2[:, oc * T + tt * 512: oc * T + tt * 512 + 512],
                            op0=ALU.mult, op1=ALU.add), sems["vec"])
                    w2free[par] = S.val(sems["mm"])

        if PH < 5:
            v_fin = S.drain("vector", sems["vec"])
            S.wait("sync", sems["vec"], v_fin)
            S.dma("sync", sems["st"], x2_d[:].rearrange("(c p) t -> p c t", p=128),
                  x2[:].rearrange("p (c t) -> p c t", c=DC))
            S.dma("sync", sems["st"], e_d[:], carry[:])
            S.wait("sync", sems["st"], S.val(sems["st"]))
            S.run(block)
            return nc
        v_bar = S.val(sems["vec"])
        S.wait("tensor", sems["vec"], v_bar)
        S.wait("scalar", sems["vec"], v_bar)
        v_st2 = emit_stats(S, sems, "s2", x2, T, 0, psA, psB, sqb, rows,
                           onescol, mbb[0:1, :T], rbb[0:1, :T])
        emit_bcast_rows(S, sems, "b2", rows, T, [mbb, rbb], [pb[4], pb[5]],
                        onesP, gate=(sems["s2"], v_st2))
        for tt in range(NT):
            for dc in range(DC):
                xo = dc * T + tt * 512
                ho = dc * 512
                S.emit("vector", lambda v, xo=xo, ho=ho, tt=tt: v.tensor_sub(
                    out=h1t[:, ho:ho + 512], in0=x2[:, xo:xo + 512],
                    in1=mbb[:, tt * 512:(tt + 1) * 512]))
                S.drain("vector")
                S.emit("vector", lambda v, ho=ho, tt=tt: v.tensor_mul(
                    out=h1t[:, ho:ho + 512], in0=h1t[:, ho:ho + 512],
                    in1=rbb[:, tt * 512:(tt + 1) * 512]))
                S.drain("vector")
                S.emit("vector", lambda v, ho=ho, dc=dc: v.tensor_scalar(
                    out=h1t[:, ho:ho + 512], in0=h1t[:, ho:ho + 512],
                    scalar1=capl[:, DC + dc:DC + dc + 1],
                    scalar2=modt1[:, 0 * DC + dc:0 * DC + dc + 1],
                    op0=ALU.mult, op1=ALU.add))
                S.drain("vector")
                S.emit("vector", lambda v, ho=ho, dc=dc: v.tensor_tensor_scan(
                    out=scant[:, 0:512], data0=at[:, dc:dc + 1].broadcast_to([128, 512]),
                    data1=h1t[:, ho:ho + 512], initial=carry[:, dc:dc + 1],
                    op0=ALU.mult, op1=ALU.add))
                S.drain("vector")
                S.emit("vector", lambda v, dc=dc: v.tensor_copy(
                    out=carry[:, dc:dc + 1], in_=scant[:, 511:512]))
                S.drain("vector")
        v_fin = S.drain("vector", sems["vec"])

        S.wait("sync", sems["vec"], v_fin)
        S.dma("sync", sems["st"], x2_d[:].rearrange("(c p) t -> p c t", p=128),
              x2[:].rearrange("p (c t) -> p c t", c=DC))
        S.dma("sync", sems["st"], e_d[:], carry[:])
        S.wait("sync", sems["st"], S.val(sems["st"]))
        S.run(block)
    return nc


def emit_scale_bcast(S, sems, mrow, n, qmax, srow, sbc, rsbc, onesP, ps):
    S.emit("vector", lambda v: v.reduce_max(out=srow[0:1, 0:1], in_=mrow[0:1, 0:n], axis=AX))
    S.drain("vector")
    S.emit("vector", lambda v: v.tensor_scalar(out=srow[0:1, 0:1], in0=srow[0:1, 0:1],
                                               scalar1=1.0 / qmax, scalar2=1e-12,
                                               op0=ALU.mult, op1=ALU.max))
    S.drain("vector")
    v = S.op("vector", lambda v: v.reciprocal(out=srow[0:1, 1:2], in_=srow[0:1, 0:1]),
             sems["vec"])
    S.wait("tensor", sems["vec"], v)
    S.op("tensor", lambda e: e.matmul(ps[:, 0:2], lhsT=onesP[:, :], rhs=srow[0:1, 0:2],
                                      start=True, stop=True), sems["mm"])
    S.wait("vector", sems["mm"], S.val(sems["mm"]))
    S.emit("vector", lambda v: v.tensor_copy(out=sbc[:], in_=ps[:, 0:1]))
    v = S.op("vector", lambda v: v.tensor_copy(out=rsbc[:], in_=ps[:, 1:2]), sems["vec"])
    S.drain("vector")
    S.wait("tensor", sems["vec"], v)
    return v


def emit_qdq(S, x, n, rsbc, sbc, lo, hi, it, t1, t2, t3, sem=None):
    S.emit("vector", lambda v: v.tensor_scalar(out=t3[:, :n], in0=x[:, :n],
                                               scalar1=rsbc[:], scalar2=None, op0=ALU.mult))
    S.drain("vector")
    emit_round(S, t3, t3, it, t1, t2, n)
    S.drain("vector")
    S.emit("vector", lambda v: v.tensor_scalar(out=t3[:, :n], in0=t3[:, :n],
                                               scalar1=float(lo), scalar2=float(hi),
                                               op0=ALU.max, op1=ALU.min))
    S.drain("vector")
    r = S.op("vector", lambda v: v.tensor_scalar(out=x[:, :n], in0=t3[:, :n],
                                                 scalar1=sbc[:], scalar2=None,
                                                 op0=ALU.mult), sem)
    S.drain("vector")
    return r


def emit_wdequant(S, wr, ncols, bm, bs, it, t1, t2, t3):
    nb = ncols // 32
    S.emit("vector", lambda v: v.reduce_max(
        out=bm[:, 0:nb, None], in_=wr[:, 0:ncols].rearrange("p (n k) -> p n k", k=32),
        axis=AX, apply_absolute_value=True))
    S.drain("vector")
    S.emit("vector", lambda v: v.tensor_scalar(out=bs[:, 0:nb], in0=bm[:, 0:nb],
                                               scalar1=1.0 / 7.0, scalar2=1e-12,
                                               op0=ALU.mult, op1=ALU.max))
    S.drain("vector")
    S.emit("vector", lambda v: v.reciprocal(out=bm[:, 0:nb], in_=bs[:, 0:nb]))
    S.drain("vector")
    S.emit("vector", lambda v: v.tensor_tensor(
        out=t3[:, 0:ncols].rearrange("p (n k) -> p n k", k=32),
        in0=wr[:, 0:ncols].rearrange("p (n k) -> p n k", k=32),
        in1=bm[:, 0:nb, None].broadcast_to([128, nb, 32]), op=ALU.mult))
    S.drain("vector")
    emit_round(S, t3, t3, it, t1, t2, ncols)
    S.drain("vector")
    S.emit("vector", lambda v: v.tensor_scalar(out=t3[:, 0:ncols], in0=t3[:, 0:ncols],
                                               scalar1=-8.0, scalar2=7.0,
                                               op0=ALU.max, op1=ALU.min))
    S.drain("vector")
    S.emit("vector", lambda v: v.tensor_tensor(
        out=wr[:, 0:ncols].rearrange("p (n k) -> p n k", k=32),
        in0=t3[:, 0:ncols].rearrange("p (n k) -> p n k", k=32),
        in1=bs[:, 0:nb, None].broadcast_to([128, nb, 32]), op=ALU.mult))
    S.drain("vector")


def emit_absmax_tile(S, src, n, mxw, t1, sem=None):
    S.emit("vector", lambda v: v.reduce_max(out=t1[:, 0:1], in_=src[:, :n], axis=AX,
                                            apply_absolute_value=True))
    S.drain("vector")
    return S.op("vector", lambda v: v.tensor_tensor(out=mxw[:], in0=mxw[:], in1=t1[:, 0:1],
                                                    op=ALU.max), sem)


def emit_mx_finalize(S, sems, mxw, out_sb, ps, ident):
    v = S.val(sems["vec"])
    S.wait("tensor", sems["vec"], v)
    S.op("tensor", lambda e: e.transpose(ps[0:1, 0:128], in_=mxw[:], identity=ident[:]),
         sems["mm"])
    S.wait("vector", sems["mm"], S.val(sems["mm"]))
    return S.op("vector", lambda v: v.reduce_max(out=out_sb[0:1, 0:1], in_=ps[0:1, 0:128],
                                                 axis=AX), sems["vec"])


def build_q1():
    nc = bass.Bass("TRN2", target_bir_lowering=False, debug=False)
    T = TOK
    I3 = 3072
    h_d = nc.dram_tensor("hq", [D, T], F32, kind="ExternalInput").ap()
    m1_d = nc.dram_tensor("m1all", [1, NCORES], F32, kind="ExternalInput").ap()
    wu_d = nc.dram_tensor("wup", [I3, D], F32, kind="ExternalInput").ap()
    wg_d = nc.dram_tensor("wgate", [I3, D], F32, kind="ExternalInput").ap()
    id_d = nc.dram_tensor("ident", [128, 128], F32, kind="ExternalInput").ap()
    up_d = nc.dram_tensor("upf", [I3, T], F32, kind="ExternalOutput").ap()
    ga_d = nc.dram_tensor("gatef", [I3, T], F32, kind="ExternalOutput").ap()
    m2_d = nc.dram_tensor("m2", [1, 1], F32, kind="ExternalOutput").ap()
    m3_d = nc.dram_tensor("m3", [1, 1], F32, kind="ExternalOutput").ap()

    from contextlib import ExitStack
    with ExitStack() as ctx:
        sb = lambda name, shape, dt=F32: ctx.enter_context(nc.sbuf_tensor(name, shape, dt))
        hq = sb("hqs", [128, DC * T])
        wr = [sb(f"wr{i}", [128, D]) for i in range(2)]
        wdqT = sb("wdqT", [128, DC * 128])
        outt = [sb(f"outt{i}", [128, 512]) for i in range(2)]
        it = ctx.enter_context(nc.sbuf_tensor("iti", [128, T], I32))
        t1 = sb("t1", [128, T])
        t2 = sb("t2", [128, T])
        t3 = sb("t3", [128, T])
        bm = sb("bm", [128, 32])
        bs = sb("bs", [128, 32])
        mrow = sb("mrow", [1, NCORES])
        srow = sb("srow", [1, 2])
        sbc = sb("sbc", [128, 1])
        rsbc = sb("rsbc", [128, 1])
        mxu = sb("mxu", [128, 1])
        mxg = sb("mxg", [128, 1])
        mfin = sb("mfin", [1, 2])
        ident = sb("identt", [128, 128])
        onesP = sb("onesP", [1, 128])
        pb = [ctx.enter_context(nc.psum_tensor(f"pb{i}", [128, 512], F32)) for i in range(4)]
        sems = mk_sems(ctx, nc, ["ld", "ldwA", "ldwB", "vec", "mm", "stA", "stB", "st"])
        block = ctx.enter_context(nc.Block())
        S = Sched(nc)

        S.dma("sync", sems["ld"], hq[:].rearrange("p (c t) -> p c t", c=DC),
              h_d[:].rearrange("(c p) t -> p c t", p=128))
        S.dma("sync", sems["ld"], mrow[:], m1_d[:])
        S.dma("sync", sems["ld"], ident[:], id_d[:])
        v_ld = S.val(sems["ld"])
        S.emit("vector", lambda v: v.memset(onesP[:], 1.0))
        S.emit("vector", lambda v: v.memset(mxu[:], 0.0))
        S.emit("vector", lambda v: v.memset(mxg[:], 0.0))
        S.wait("vector", sems["ld"], v_ld)
        S.wait("tensor", sems["ld"], v_ld)
        emit_scale_bcast(S, sems, mrow, NCORES, 32767.0, srow, sbc, rsbc, onesP, pb[3])
        for dc in range(DC):
            emit_qdq(S, hq[:, dc * T:(dc + 1) * T], T, rsbc, sbc, -32768, 32767,
                     it, t1, t2, t3)
        v_qx = S.drain("vector", sems["vec"])
        S.wait("tensor", sems["vec"], v_qx)
        S.wait("sync", sems["vec"], v_qx)

        wfree = {0: 0, 1: 0}
        stfree = {0: 0, 1: 0}
        nst = 0
        for wi, (w_dd, o_dd, mxw) in enumerate([(wu_d, up_d, mxu), (wg_d, ga_d, mxg)]):
            for rc in range(24):
                par = (wi * 24 + rc) % 2
                sld = sems["ldwA" if par == 0 else "ldwB"]
                S.wait("sync", sems["mm"], wfree[par])
                S.dma("sync", sld, wr[par][:].rearrange("p (c q) -> p c q", c=DC),
                      w_dd[rc * 128:(rc + 1) * 128, :].rearrange("p (c q) -> p c q", c=DC))
                v_lw = S.val(sld)
                S.wait("vector", sld, v_lw)
                emit_wdequant(S, wr[par], D, bm, bs, it, t1, t2, t3)
                v_dq = S.drain("vector", sems["vec"])
                S.wait("tensor", sems["vec"], v_dq)
                for hc in range(DC):
                    S.op("tensor", lambda e, hc=hc, par=par: e.transpose(
                        pb[2][:, 0:128], in_=wr[par][:, hc * 128:(hc + 1) * 128],
                        identity=ident[:]), sems["mm"])
                    S.wait("vector", sems["mm"], S.val(sems["mm"]))
                    v = S.op("vector", lambda v, hc=hc: v.tensor_copy(
                        out=wdqT[:, hc * 128:(hc + 1) * 128], in_=pb[2][:, 0:128]),
                        sems["vec"])
                    S.wait("tensor", sems["vec"], v)
                wfree[par] = S.val(sems["mm"])
                for tt in range(T // 512):
                    pbank = pb[tt % 2]
                    sp = (wi * 24 + rc) * 4 + tt
                    spar = sp % 2
                    sst = sems["stA" if spar == 0 else "stB"]
                    for dc in range(DC - 1):
                        S.emit("tensor", lambda e, dc=dc, tt=tt, pbank=pbank: e.matmul(
                            pbank[:, 0:512], lhsT=wdqT[:, dc * 128:(dc + 1) * 128],
                            rhs=hq[:, dc * T + tt * 512: dc * T + tt * 512 + 512],
                            start=(dc == 0), stop=False))
                    S.op("tensor", lambda e, tt=tt, pbank=pbank: e.matmul(
                        pbank[:, 0:512], lhsT=wdqT[:, (DC - 1) * 128:DC * 128],
                        rhs=hq[:, (DC - 1) * T + tt * 512: (DC - 1) * T + tt * 512 + 512],
                        start=False, stop=True), sems["mm"])
                    S.wait("vector", sems["mm"], S.val(sems["mm"]))
                    S.wait("vector", sst, stfree[spar])
                    S.emit("vector", lambda v, spar=spar, pbank=pbank: v.tensor_copy(
                        out=outt[spar][:], in_=pbank[:, 0:512]))
                    emit_absmax_tile(S, pbank, 512, mxw, bm)
                    v_e = S.drain("vector", sems["vec"])
                    S.wait("tensor", sems["vec"], v_e)
                    S.wait("sync", sems["vec"], v_e)
                    S.dma("sync", sst, o_dd[rc * 128:(rc + 1) * 128, tt * 512:(tt + 1) * 512],
                          outt[spar][:])
                    stfree[spar] = S.val(sst)
                    nst += 1
        emit_mx_finalize(S, sems, mxu, mfin[0:1, 0:1], pb[3], ident)
        v_f = emit_mx_finalize(S, sems, mxg, mfin[0:1, 1:2], pb[3], ident)
        S.wait("sync", sems["vec"], v_f)
        S.dma("sync", sems["st"], m2_d[:], mfin[0:1, 0:1])
        S.dma("sync", sems["st"], m3_d[:], mfin[0:1, 1:2])
        S.wait("sync", sems["st"], S.val(sems["st"]))
        S.wait("sync", sems["stA"], S.val(sems["stA"]))
        S.wait("sync", sems["stB"], S.val(sems["stB"]))
        S.run(block)
    return nc


def build_q2():
    nc = bass.Bass("TRN2", target_bir_lowering=False, debug=False)
    T = TOK
    I3 = 3072
    up_d = nc.dram_tensor("upf", [I3, T], F32, kind="ExternalInput").ap()
    ga_d = nc.dram_tensor("gatef", [I3, T], F32, kind="ExternalInput").ap()
    m2_d = nc.dram_tensor("m2all", [1, NCORES], F32, kind="ExternalInput").ap()
    m3_d = nc.dram_tensor("m3all", [1, NCORES], F32, kind="ExternalInput").ap()
    id_d = nc.dram_tensor("ident", [128, 128], F32, kind="ExternalInput").ap()
    qu_d = nc.dram_tensor("qupf", [I3, T], F32, kind="ExternalOutput").ap()
    pr_d = nc.dram_tensor("prodf", [I3, T], F32, kind="ExternalOutput").ap()
    m4_d = nc.dram_tensor("m4", [1, 1], F32, kind="ExternalOutput").ap()

    from contextlib import ExitStack
    with ExitStack() as ctx:
        sb = lambda name, shape, dt=F32: ctx.enter_context(nc.sbuf_tensor(name, shape, dt))
        ut = [sb(f"ut{i}", [128, T]) for i in range(2)]
        gt = [sb(f"gt{i}", [128, T]) for i in range(2)]
        sg = sb("sg", [128, T])
        it = ctx.enter_context(nc.sbuf_tensor("iti", [128, T], I32))
        t1 = sb("t1", [128, T])
        t2 = sb("t2", [128, T])
        t3 = sb("t3", [128, T])
        mrow = sb("mrow", [1, 2 * NCORES])
        srow = sb("srow", [1, 4])
        s2bc = sb("s2bc", [128, 1])
        rs2bc = sb("rs2bc", [128, 1])
        s3bc = sb("s3bc", [128, 1])
        rs3bc = sb("rs3bc", [128, 1])
        mx4 = sb("mx4", [128, 1])
        mfin = sb("mfin", [1, 1])
        ident = sb("identt", [128, 128])
        onesP = sb("onesP", [1, 128])
        tb = sb("tb", [128, 1])
        pb = [ctx.enter_context(nc.psum_tensor(f"pb{i}", [128, 512], F32)) for i in range(1)]
        sems = mk_sems(ctx, nc, ["ld", "ldA", "ldB", "vec", "mm", "stA", "stB", "st"])
        block = ctx.enter_context(nc.Block())
        S = Sched(nc)

        S.dma("sync", sems["ld"], mrow[0:1, 0:NCORES], m2_d[:])
        S.dma("sync", sems["ld"], mrow[0:1, NCORES:2 * NCORES], m3_d[:])
        S.dma("sync", sems["ld"], ident[:], id_d[:])
        v_ld = S.val(sems["ld"])
        S.emit("vector", lambda v: v.memset(onesP[:], 1.0))
        S.emit("vector", lambda v: v.memset(mx4[:], 0.0))
        S.wait("vector", sems["ld"], v_ld)
        S.wait("tensor", sems["ld"], v_ld)
        emit_scale_bcast(S, sems, mrow[0:1, 0:NCORES], NCORES, 32767.0, srow[0:1, 0:2],
                         s2bc, rs2bc, onesP, pb[0])
        emit_scale_bcast(S, sems, mrow[0:1, NCORES:2 * NCORES], NCORES, 32767.0,
                         srow[0:1, 2:4], s3bc, rs3bc, onesP, pb[0])

        ldfree = {0: 0, 1: 0}
        stdone = {0: 0, 1: 0}
        for rc in range(24):
            par = rc % 2
            sld = sems["ldA" if par == 0 else "ldB"]
            sst = sems["stA" if par == 0 else "stB"]
            S.wait("sync", sems["vec"], ldfree[par])
            S.wait("sync", sst, stdone[par])
            S.dma("sync", sld, ut[par][:], up_d[rc * 128:(rc + 1) * 128, :])
            S.dma("sync", sld, gt[par][:], ga_d[rc * 128:(rc + 1) * 128, :])
            v_lw = S.val(sld)
            S.wait("vector", sld, v_lw)
            S.wait("vector", sst, stdone[par])
            emit_qdq(S, ut[par], T, rs2bc, s2bc, -32768, 32767, it, t1, t2, t3)
            emit_qdq(S, gt[par], T, rs3bc, s3bc, -32768, 32767, it, t1, t2, t3)
            v_q = S.drain("vector", sems["vec"])
            S.wait("scalar", sems["vec"], v_q)
            v_s = S.op("scalar", lambda s, par=par: s.activation(
                out=sg[:], in_=gt[par][:], func=AF.Sigmoid), sems["mm"])
            S.wait("vector", sems["mm"], v_s)
            S.emit("vector", lambda v: v.tensor_scalar(out=sg[:], in0=sg[:],
                                                       scalar1=65536.0, scalar2=None,
                                                       op0=ALU.mult))
            S.drain("vector")
            emit_round(S, sg, sg, it, t1, t2, T)
            S.drain("vector")
            S.emit("vector", lambda v: v.tensor_scalar(out=sg[:], in0=sg[:],
                                                       scalar1=0.0, scalar2=65535.0,
                                                       op0=ALU.max, op1=ALU.min))
            S.drain("vector")
            S.emit("vector", lambda v: v.tensor_scalar(out=sg[:], in0=sg[:],
                                                       scalar1=1.0 / 65536.0, scalar2=None,
                                                       op0=ALU.mult))
            S.drain("vector")
            S.emit("vector", lambda v, par=par: v.tensor_mul(out=gt[par][:],
                                                             in0=gt[par][:], in1=sg[:]))
            S.drain("vector")
            emit_absmax_tile(S, gt[par], T, mx4, tb)
            v_e = S.drain("vector", sems["vec"])
            ldfree[par] = v_e
            S.wait("sync", sems["vec"], v_e)
            S.dma("sync", sst, qu_d[rc * 128:(rc + 1) * 128, :], ut[par][:])
            S.dma("sync", sst, pr_d[rc * 128:(rc + 1) * 128, :], gt[par][:])
            stdone[par] = S.val(sst)
        v_f = emit_mx_finalize(S, sems, mx4, mfin[0:1, 0:1], pb[0], ident)
        S.wait("sync", sems["vec"], v_f)
        S.dma("sync", sems["st"], m4_d[:], mfin[0:1, 0:1])
        S.wait("sync", sems["st"], S.val(sems["st"]))
        S.wait("sync", sems["stA"], S.val(sems["stA"]))
        S.wait("sync", sems["stB"], S.val(sems["stB"]))
        S.run(block)
    return nc


def build_q3():
    nc = bass.Bass("TRN2", target_bir_lowering=False, debug=False)
    T = TOK
    I3 = 3072
    pr_d = nc.dram_tensor("prodf", [I3, T], F32, kind="ExternalInput").ap()
    qu_d = nc.dram_tensor("qupf", [I3, T], F32, kind="ExternalInput").ap()
    m4_d = nc.dram_tensor("m4all", [1, NCORES], F32, kind="ExternalInput").ap()
    id_d = nc.dram_tensor("ident", [128, 128], F32, kind="ExternalInput").ap()
    p2_d = nc.dram_tensor("prod2f", [I3, T], F32, kind="ExternalOutput").ap()
    m5_d = nc.dram_tensor("m5", [1, 1], F32, kind="ExternalOutput").ap()

    from contextlib import ExitStack
    with ExitStack() as ctx:
        sb = lambda name, shape, dt=F32: ctx.enter_context(nc.sbuf_tensor(name, shape, dt))
        pt = [sb(f"pt{i}", [128, T]) for i in range(2)]
        qt = [sb(f"qt{i}", [128, T]) for i in range(2)]
        it = ctx.enter_context(nc.sbuf_tensor("iti", [128, T], I32))
        t1 = sb("t1", [128, T])
        t2 = sb("t2", [128, T])
        t3 = sb("t3", [128, T])
        mrow = sb("mrow", [1, NCORES])
        srow = sb("srow", [1, 2])
        s4bc = sb("s4bc", [128, 1])
        rs4bc = sb("rs4bc", [128, 1])
        mx5 = sb("mx5", [128, 1])
        mfin = sb("mfin", [1, 1])
        ident = sb("identt", [128, 128])
        onesP = sb("onesP", [1, 128])
        tb = sb("tb", [128, 1])
        pb = [ctx.enter_context(nc.psum_tensor(f"pb{i}", [128, 512], F32)) for i in range(1)]
        sems = mk_sems(ctx, nc, ["ld", "ldA", "ldB", "vec", "mm", "stA", "stB", "st"])
        block = ctx.enter_context(nc.Block())
        S = Sched(nc)

        S.dma("sync", sems["ld"], mrow[:], m4_d[:])
        S.dma("sync", sems["ld"], ident[:], id_d[:])
        v_ld = S.val(sems["ld"])
        S.emit("vector", lambda v: v.memset(onesP[:], 1.0))
        S.emit("vector", lambda v: v.memset(mx5[:], 0.0))
        S.wait("vector", sems["ld"], v_ld)
        S.wait("tensor", sems["ld"], v_ld)
        emit_scale_bcast(S, sems, mrow, NCORES, 32767.0, srow, s4bc, rs4bc, onesP, pb[0])
        ldfree = {0: 0, 1: 0}
        stdone = {0: 0, 1: 0}
        for rc in range(24):
            par = rc % 2
            sld = sems["ldA" if par == 0 else "ldB"]
            sst = sems["stA" if par == 0 else "stB"]
            S.wait("sync", sems["vec"], ldfree[par])
            S.wait("sync", sst, stdone[par])
            S.dma("sync", sld, pt[par][:], pr_d[rc * 128:(rc + 1) * 128, :])
            S.dma("sync", sld, qt[par][:], qu_d[rc * 128:(rc + 1) * 128, :])
            v_lw = S.val(sld)
            S.wait("vector", sld, v_lw)
            S.wait("vector", sst, stdone[par])
            emit_qdq(S, pt[par], T, rs4bc, s4bc, -32768, 32767, it, t1, t2, t3)
            S.emit("vector", lambda v, par=par: v.tensor_mul(out=pt[par][:],
                                                             in0=pt[par][:], in1=qt[par][:]))
            S.drain("vector")
            emit_absmax_tile(S, pt[par], T, mx5, tb)
            v_e = S.drain("vector", sems["vec"])
            ldfree[par] = v_e
            S.wait("sync", sems["vec"], v_e)
            S.dma("sync", sst, p2_d[rc * 128:(rc + 1) * 128, :], pt[par][:])
            stdone[par] = S.val(sst)
        v_f = emit_mx_finalize(S, sems, mx5, mfin[0:1, 0:1], pb[0], ident)
        S.wait("sync", sems["vec"], v_f)
        S.dma("sync", sems["st"], m5_d[:], mfin[0:1, 0:1])
        S.wait("sync", sems["st"], S.val(sems["st"]))
        S.wait("sync", sems["stA"], S.val(sems["stA"]))
        S.wait("sync", sems["stB"], S.val(sems["stB"]))
        S.run(block)
    return nc


def build_q4():
    nc = bass.Bass("TRN2", target_bir_lowering=False, debug=False)
    T = TOK
    I3 = 3072
    IC = 24
    p2_d = nc.dram_tensor("prod2f", [I3, T], F32, kind="ExternalInput").ap()
    m5_d = nc.dram_tensor("m5all", [1, NCORES], F32, kind="ExternalInput").ap()
    wd_d = nc.dram_tensor("wdown", [D, I3], F32, kind="ExternalInput").ap()
    x3_d = nc.dram_tensor("x3f", [D, T], F32, kind="ExternalInput").ap()
    md_d = nc.dram_tensor("mod1f", [6, 128, DC], F32, kind="ExternalInput").ap()
    id_d = nc.dram_tensor("ident", [128, 128], F32, kind="ExternalInput").ap()
    xo_d = nc.dram_tensor("xof", [D, T], F32, kind="ExternalOutput").ap()

    from contextlib import ExitStack
    with ExitStack() as ctx:
        sb = lambda name, shape, dt=F32: ctx.enter_context(nc.sbuf_tensor(name, shape, dt))
        wdT = sb("wdT", [128, IC * 1024])
        ot = sb("ot", [128, IC * 512])
        wdr = [sb(f"wdr{i}", [128, I3]) for i in range(2)]
        x3t = sb("x3t", [128, DC * 512])
        xot = [sb(f"xot{i}", [128, 512]) for i in range(2)]
        it = ctx.enter_context(nc.sbuf_tensor("iti", [128, 1024], I32))
        t1 = sb("t1", [128, 1024])
        t2 = sb("t2", [128, 1024])
        t3 = sb("t3", [128, 1024])
        bm = sb("bm", [128, 96])
        bs = sb("bs", [128, 96])
        mrow = sb("mrow", [1, NCORES])
        srow = sb("srow", [1, 2])
        s5bc = sb("s5bc", [128, 1])
        rs5bc = sb("rs5bc", [128, 1])
        modt = sb("modt", [128, 6 * DC])
        ident = sb("identt", [128, 128])
        onesP = sb("onesP", [1, 128])
        pb = [ctx.enter_context(nc.psum_tensor(f"pb{i}", [128, 512], F32)) for i in range(4)]
        sems = mk_sems(ctx, nc, ["ld", "ldwA", "ldwB", "ldp", "ldx", "vec", "mm",
                                 "stA", "stB"])
        block = ctx.enter_context(nc.Block())
        S = Sched(nc)

        S.dma("sync", sems["ld"], mrow[:], m5_d[:])
        S.dma("sync", sems["ld"], ident[:], id_d[:])
        S.dma("sync", sems["ld"], modt[:].rearrange("p (m c) -> p m c", m=6),
              md_d[:].rearrange("m p c -> p m c"))
        v_ld = S.val(sems["ld"])
        S.emit("vector", lambda v: v.memset(onesP[:], 1.0))
        S.wait("vector", sems["ld"], v_ld)
        S.wait("tensor", sems["ld"], v_ld)
        emit_scale_bcast(S, sems, mrow, NCORES, 32767.0, srow, s5bc, rs5bc, onesP, pb[3])

        wfree = {0: 0, 1: 0}
        for oc in range(DC):
            par = oc % 2
            sld = sems["ldwA" if par == 0 else "ldwB"]
            S.wait("sync", sems["mm"], wfree[par])
            S.dma("sync", sld, wdr[par][:], wd_d[oc * 128:(oc + 1) * 128, :])
            v_lw = S.val(sld)
            S.wait("vector", sld, v_lw)
            for sl in range(3):
                emit_wdequant(S, wdr[par][:, sl * 1024:(sl + 1) * 1024], 1024,
                              bm, bs, it, t1, t2, t3)
            v_dq = S.drain("vector", sems["vec"])
            S.wait("tensor", sems["vec"], v_dq)
            for icc in range(IC):
                S.op("tensor", lambda e, icc=icc, par=par: e.transpose(
                    pb[2][:, 0:128], in_=wdr[par][:, icc * 128:(icc + 1) * 128],
                    identity=ident[:]), sems["mm"])
                S.wait("vector", sems["mm"], S.val(sems["mm"]))
                v = S.op("vector", lambda v, icc=icc, oc=oc: v.tensor_copy(
                    out=wdT[:, icc * 1024 + oc * 128: icc * 1024 + oc * 128 + 128],
                    in_=pb[2][:, 0:128]), sems["vec"])
                S.wait("tensor", sems["vec"], v)
            wfree[par] = S.val(sems["mm"])

        stdone = {0: 0, 1: 0}
        for tt in range(T // 512):
            S.wait("sync", sems["vec"], S.val(sems["vec"]))
            S.dma("sync", sems["ldp"], ot[:].rearrange("p (c t) -> p c t", c=IC),
                  p2_d[:, tt * 512:(tt + 1) * 512].rearrange("(c p) t -> p c t", p=128))
            S.dma("sync", sems["ldx"], x3t[:].rearrange("p (c t) -> p c t", c=DC),
                  x3_d[:, tt * 512:(tt + 1) * 512].rearrange("(c p) t -> p c t", p=128))
            v_lp = S.val(sems["ldp"])
            v_lx = S.val(sems["ldx"])
            S.wait("vector", sems["ldp"], v_lp)
            for icc in range(IC):
                emit_qdq(S, ot[:, icc * 512:(icc + 1) * 512], 512, rs5bc, s5bc,
                         -32768, 32767, it, t1, t2, t3)
            v_o = S.drain("vector", sems["vec"])
            S.wait("tensor", sems["vec"], v_o)
            S.wait("vector", sems["ldx"], v_lx)
            for oc in range(DC):
                par = oc % 2
                pbank = pb[oc % 2]
                sst = sems["stA" if par == 0 else "stB"]
                for icc in range(IC - 1):
                    S.emit("tensor", lambda e, icc=icc, oc=oc, pbank=pbank: e.matmul(
                        pbank[:, 0:512],
                        lhsT=wdT[:, icc * 1024 + oc * 128: icc * 1024 + oc * 128 + 128],
                        rhs=ot[:, icc * 512:(icc + 1) * 512],
                        start=(icc == 0), stop=False))
                S.op("tensor", lambda e, oc=oc, pbank=pbank: e.matmul(
                    pbank[:, 0:512],
                    lhsT=wdT[:, (IC - 1) * 1024 + oc * 128: (IC - 1) * 1024 + oc * 128 + 128],
                    rhs=ot[:, (IC - 1) * 512: IC * 512], start=False, stop=True), sems["mm"])
                S.wait("vector", sems["mm"], S.val(sems["mm"]))
                S.wait("vector", sst, stdone[par])
                v_e = S.op("vector", lambda v, oc=oc, par=par, pbank=pbank: v.scalar_tensor_tensor(
                    out=xot[par][:], in0=pbank[:, 0:512],
                    scalar=modt[:, 5 * DC + oc:5 * DC + oc + 1],
                    in1=x3t[:, oc * 512:(oc + 1) * 512], op0=ALU.mult, op1=ALU.add),
                    sems["vec"])
                S.wait("tensor", sems["vec"], v_e)
                S.wait("sync", sems["vec"], v_e)
                S.dma("sync", sst, xo_d[oc * 128:(oc + 1) * 128, tt * 512:(tt + 1) * 512],
                      xot[par][:])
                stdone[par] = S.val(sst)
        S.wait("sync", sems["stA"], S.val(sems["stA"]))
        S.wait("sync", sems["stB"], S.val(sems["stB"]))
        S.run(block)
    return nc


_NC_CACHE = {}


def _get(name, builder):
    if name not in _NC_CACHE:
        _NC_CACHE[name] = builder()
    return _NC_CACHE[name]


def _run(name, builder, in_maps):
    nc = _get(name, builder)
    return run_bass_kernel_spmd(nc, in_maps, core_ids=list(range(NCORES))).results


def kernel(**inputs):
    ident = np.eye(128, dtype=np.float32)
    x = np.asarray(inputs["L0_attn_x"], np.float32)
    xT = np.ascontiguousarray(x.reshape(SEQ, D).T)

    c0 = np.asarray(inputs["L0_c"], np.float32)
    c1 = np.asarray(inputs["L1_c"], np.float32)
    w0 = np.asarray(inputs["L0_wmod"], np.float32)
    w1m = np.asarray(inputs["L1_wmod"], np.float32)
    mm_in = []
    for i in range(NCORES):
        c, w = (c0, w0) if i % 2 == 0 else (c1, w1m)
        mm_in.append({"cf": _fm(c[0]), "wmod": np.ascontiguousarray(w)})
    r = _run("mods", build_mods, mm_in)
    mod0f = np.ascontiguousarray(r[0]["modf"])
    mod1f = np.ascontiguousarray(r[1]["modf"])

    maskc = np.zeros((128, 128), np.float32)
    for j in range(128):
        maskc[j, j:] = 1.0
    a_in = []
    for i in range(NCORES):
        xf = np.zeros((D, TOK + 128), np.float32)
        if i > 0:
            xf[:, :128] = xT[:, i * TOK - 128: i * TOK]
        xf[:, 128:] = xT[:, i * TOK: (i + 1) * TOK]
        maskp = (np.zeros if i == 0 else np.ones)((128, 128), np.float32)
        a_in.append({
            "xf": xf,
            "wq": np.ascontiguousarray(inputs["L0_attn_Wq"], np.float32),
            "wk": np.ascontiguousarray(inputs["L0_attn_Wk"], np.float32),
            "wv": np.ascontiguousarray(inputs["L0_attn_Wv"], np.float32),
            "wo": np.ascontiguousarray(inputs["L0_attn_Wo"], np.float32),
            "bqf": _fm(inputs["L0_attn_bq"]), "bkf": _fm(inputs["L0_attn_bk"]),
            "bof": _fm(inputs["L0_attn_bo"]),
            "bvrow": np.asarray(inputs["L0_attn_bv"], np.float32).reshape(1, D),
            "mod0f": mod0f, "ident": ident, "maskp": maskp, "maskc": maskc,
        })
    r = _run("attn", build_attn, a_in)
    x1s = [np.ascontiguousarray(ri["x1f"]) for ri in r]

    onehot = np.zeros((128, 1024), np.float32)
    for e in range(8):
        onehot[e, e * 128:(e + 1) * 128] = 1.0
    A = np.asarray(inputs["L1_attn_A"], np.float32)
    m_in = []
    for i in range(NCORES):
        m_in.append({
            "x1f": x1s[i], "mod0f": mod0f, "mod1f": mod1f,
            "gwT": np.ascontiguousarray(np.asarray(inputs["L0_ffn_gate_w"], np.float32).T),
            "gbrow": np.asarray(inputs["L0_ffn_gate_bias"], np.float32).reshape(1, 8),
            "onehot": onehot,
            "w1T": np.ascontiguousarray(np.asarray(inputs["L0_ffn_w1"], np.float32).transpose(0, 2, 1)),
            "w3T": np.ascontiguousarray(np.asarray(inputs["L0_ffn_w3"], np.float32).transpose(0, 2, 1)),
            "w2T": np.ascontiguousarray(np.asarray(inputs["L0_ffn_w2"], np.float32).transpose(0, 2, 1)),
            "sw1T": np.ascontiguousarray(np.asarray(inputs["L0_ffn_sw1"], np.float32).T),
            "sw3T": np.ascontiguousarray(np.asarray(inputs["L0_ffn_sw3"], np.float32).T),
            "sw2T": np.ascontiguousarray(np.asarray(inputs["L0_ffn_sw2"], np.float32).T),
            "Afm": _fm(A), "ident": ident,
        })
    r = _run("moe", build_moe, m_in)
    x2s = [np.ascontiguousarray(ri["x2f"]) for ri in r]
    E_all = np.stack([ri["Ei"] for ri in r])

    afm = _fm(A)
    l2_in = []
    for i in range(NCORES):
        eslot = np.zeros((NCORES, 128, DC), np.float32)
        for j in range(i):
            eslot[8 - i + j] = E_all[j]
        l2_in.append({
            "x1T": x2s[i], "Eslot": eslot, "mod1f": mod1f, "Afm": afm,
            "BC": np.ascontiguousarray(inputs["L1_attn_BC"], np.float32),
            "ident": ident,
        })
    r = _run("l2", build_L2, l2_in)
    x3s = [np.ascontiguousarray(ri["x2T"]) for ri in r]
    hqs = [np.ascontiguousarray(ri["h2T"]) for ri in r]
    m1all = np.array([[ri["m1"][0, 0] for ri in r]], np.float32)

    q1_in = [{"hq": hqs[i], "m1all": m1all,
              "wup": np.ascontiguousarray(inputs["L1_ffn_w_up"], np.float32),
              "wgate": np.ascontiguousarray(inputs["L1_ffn_w_gate"], np.float32),
              "ident": ident} for i in range(NCORES)]
    r = _run("q1", build_q1, q1_in)
    ups = [np.ascontiguousarray(ri["upf"]) for ri in r]
    gas = [np.ascontiguousarray(ri["gatef"]) for ri in r]
    m2all = np.array([[ri["m2"][0, 0] for ri in r]], np.float32)
    m3all = np.array([[ri["m3"][0, 0] for ri in r]], np.float32)

    q2_in = [{"upf": ups[i], "gatef": gas[i], "m2all": m2all, "m3all": m3all,
              "ident": ident} for i in range(NCORES)]
    r = _run("q2", build_q2, q2_in)
    qus = [np.ascontiguousarray(ri["qupf"]) for ri in r]
    prs = [np.ascontiguousarray(ri["prodf"]) for ri in r]
    m4all = np.array([[ri["m4"][0, 0] for ri in r]], np.float32)

    q3_in = [{"prodf": prs[i], "qupf": qus[i], "m4all": m4all,
              "ident": ident} for i in range(NCORES)]
    r = _run("q3", build_q3, q3_in)
    p2s = [np.ascontiguousarray(ri["prod2f"]) for ri in r]
    m5all = np.array([[ri["m5"][0, 0] for ri in r]], np.float32)

    q4_in = [{"prod2f": p2s[i], "m5all": m5all,
              "wdown": np.ascontiguousarray(inputs["L1_ffn_w_down"], np.float32),
              "x3f": x3s[i], "mod1f": mod1f, "ident": ident}
             for i in range(NCORES)]
    r = _run("q4", build_q4, q4_in)
    out = np.concatenate([ri["xof"] for ri in r], axis=1)
    return np.ascontiguousarray(out.T).reshape(1, SEQ, D).astype(np.float32)
